# version 28
# baseline (speedup 1.0000x reference)
import contextlib
import numpy as np
import ml_dtypes
import concourse.bass as bass
import concourse.mybir as mybir
from concourse.bass_utils import run_bass_kernel_spmd

F32 = mybir.dt.float32
BF16 = mybir.dt.bfloat16
I32 = mybir.dt.int32
OP = mybir.AluOpType
AF = mybir.ActivationFunctionType
AX = mybir.AxisListType
BF = ml_dtypes.bfloat16

B, S, D, DI, DH, NS = 4, 4096, 1024, 2048, 4096, 16
DHALF = DI // 2
THALF = S // 2
TCA = 512
NCA = S // TCA
NTILE = TCA // 128
NG = 8
PAD = 64
TB = THALF + 2 * PAD
SP = S + 2 * PAD
TCB = 256
NCB = THALF // TCB
TW = TCB + 2
EPS = 1e-5

_DEBUG = False
_STUB_EXCHANGE = False
_dbg = {}


class _Rec:
    def __init__(self):
        self.calls = []

    def __getattr__(self, name):
        def call(*a, **k):
            self.calls.append((name, a, k))
            return self
        return call

    def then_inc(self, *a, **k):
        return self


def _is_ap(v):
    return isinstance(v, bass.AP)


class Prog:

    ENGS = ("sync", "vector", "scalar", "tensor", "gpsimd")

    def __init__(self, name):
        self.nc = bass.Bass("TRN2", target_bir_lowering=False, num_devices=8,
                            detect_race_conditions=False)
        self.items = []
        self.hist = {}
        self.in_maps = [dict() for _ in range(8)]
        self.es = contextlib.ExitStack()
        self._ctr = 0
        self.name = name
        self._rowlen = {}
        self._fence = None

    def din(self, name, arrays):
        a0 = np.ascontiguousarray(arrays[0])
        dt = {np.dtype(np.float32): F32, np.dtype(BF): BF16,
              np.dtype(np.int32): I32}[a0.dtype]
        for i in range(8):
            self.in_maps[i][name] = np.ascontiguousarray(arrays[i])
        return self.nc.dram_tensor(name, list(a0.shape), dt, kind="ExternalInput").ap()

    def dout(self, name, shape, dt=F32):
        return self.nc.dram_tensor(name, list(shape), dt, kind="ExternalOutput").ap()

    def dtmp(self, name, shape, dt=F32, shared=False):
        return self.nc.dram_tensor(
            name, list(shape), dt, kind="Internal",
            addr_space="Shared" if shared else "Local").ap()

    def sb(self, shape, dt=F32, es=None):
        self._ctr += 1
        return (es or self.es).enter_context(
            self.nc.sbuf_tensor(f"sb{self._ctr}", shape, dt))

    def ps(self, shape, dt=F32, es=None):
        self._ctr += 1
        return (es or self.es).enter_context(
            self.nc.psum_tensor(f"ps{self._ctr}", shape, dt))

    def _span(self, ap):
        t = ap.tensor
        name = t.name
        dims = ap.ap
        off = ap.offset
        sp = str(ap.space)
        if "SBUF" in sp or "PSUM" in sp:
            rl = self._rowlen.get(name)
            if rl is None:
                shp = list(t.shape)
                rl = 1
                for s in shp[1:]:
                    rl *= s
                self._rowlen[name] = rl
            p0 = off // rl if rl else 0
            f0 = off - p0 * rl
            pc = dims[0][1] if dims else 1
            f1 = f0 + sum(st * (ct - 1) for st, ct in dims[1:]) + 1
            return (name, p0, p0 + pc, f0, f1)
        f1 = off + sum(st * (ct - 1) for st, ct in dims) + 1
        return (name, 0, 128, off, f1)

    @staticmethod
    def _overlap(a, b):
        return a[1] < b[2] and b[1] < a[2] and a[3] < b[4] and b[3] < a[4]

    @staticmethod
    def _contains(outer, inner):
        return (outer[1] <= inner[1] and outer[2] >= inner[2]
                and outer[3] <= inner[3] and outer[4] >= inner[4])

    def _classify(self, name, a, k):
        if name == "dma_start":
            out = k.get("out", a[0] if a else None)
            in_ = k.get("in_", a[1] if len(a) > 1 else None)
            return [out], ([in_] if _is_ap(in_) else [])
        pos = [v for v in a if _is_ap(v)]
        writes = pos[:1]
        reads = pos[1:]
        for kk, v in k.items():
            if not _is_ap(v):
                continue
            if kk == "accum_out":
                writes.append(v)
            else:
                reads.append(v)
        return writes, reads

    def step(self, engine, fn, inc=1):
        rec = _Rec()
        fn(rec)
        assert len(rec.calls) == 1, f"step recorded {len(rec.calls)} calls"
        name, a, k = rec.calls[0]
        writes, reads = self._classify(name, a, k)
        idx = len(self.items)
        deps = set()
        if self._fence is not None:
            deps.add(self._fence)
        for ap in reads:
            s = self._span(ap)
            for ent in reversed(self.hist.get(s[0], ())):
                if ent[1] and self._overlap(s, ent[2]):
                    deps.add(ent[0])
                    if self._contains(ent[2], s):
                        break
        for ap in writes:
            s = self._span(ap)
            for ent in reversed(self.hist.get(s[0], ())):
                if self._overlap(s, ent[2]):
                    deps.add(ent[0])
                    if ent[1] and self._contains(ent[2], s):
                        break
        for ap in reads:
            s = self._span(ap)
            self.hist.setdefault(s[0], []).append((idx, False, s))
        for ap in writes:
            s = self._span(ap)
            self.hist.setdefault(s[0], []).append((idx, True, s))
        self.items.append(dict(engine=engine, call=(name, a, k), inc=inc,
                               deps=deps, raw=None, is_dma=(name == "dma_start")))

    def raw(self, engine, emitter, inc):
        idx = len(self.items)
        deps = set(range(idx))
        self.items.append(dict(engine=engine, call=None, inc=inc, deps=deps,
                               raw=emitter, is_dma=False))
        self._fence = idx

    def dma(self, dst, src, eng="sync"):
        self.step(eng, (lambda dst=dst, src=src:
                        (lambda e: e.dma_start(out=dst, in_=src)))(), 16)

    def run(self):
        nc = self.nc
        NLANES = 6
        lane_count = {}
        lane_prev = {}
        eng_cum = {e: 0 for e in self.ENGS}
        qpos = {}
        n = len(self.items)
        semkey = [None] * n
        vafter = [0] * n
        selfdep = [None] * n
        incs = [0] * n
        needed = [False] * n
        for it in self.items:
            for d in it["deps"]:
                needed[d] = True
        last_per_eng = {}
        for i, it in enumerate(self.items):
            last_per_eng[it["engine"]] = i
        for i in last_per_eng.values():
            needed[i] = True
        for i, it in enumerate(self.items):
            if it["raw"] is not None or it["is_dma"]:
                needed[i] = True
        for i, it in enumerate(self.items):
            e = it["engine"]
            if it["raw"] is not None:
                incs[i] = it["inc"]
                eng_cum[e] += incs[i]
                semkey[i] = e
                vafter[i] = eng_cum[e]
            elif it["is_dma"]:
                j = qpos.get(e, 0) % NLANES
                qpos[e] = qpos.get(e, 0) + 1
                key = (e, j)
                selfdep[i] = lane_prev.get(key)
                lane_prev[key] = i
                lane_count[key] = lane_count.get(key, 0) + 16
                semkey[i] = key
                incs[i] = 16
                vafter[i] = lane_count[key]
            else:
                inc = it["inc"] if needed[i] else 0
                incs[i] = inc
                eng_cum[e] += inc
                semkey[i] = e
                vafter[i] = eng_cum[e]
        totals = dict(eng_cum)
        lane_totals = dict(lane_count)
        with contextlib.ExitStack() as semstack:
            sems = {e: semstack.enter_context(nc.semaphore(f"sem_{e}"))
                    for e in self.ENGS}
            for key in lane_totals:
                sems[key] = semstack.enter_context(
                    nc.semaphore(f"sem_{key[0]}_l{key[1]}"))
            with nc.Block() as block:
                def mk(engname):
                    def _(eng):
                        hwm = {}

                        def wait(key, v):
                            if hwm.get(key, 0) >= v:
                                return
                            eng.wait_ge(sems[key], v)
                            hwm[key] = v
                        for i, it in enumerate(self.items):
                            if it["engine"] != engname:
                                continue
                            want = {}
                            deps = it["deps"]
                            if selfdep[i] is not None:
                                deps = deps | {selfdep[i]}
                            for d in deps:
                                key = semkey[d]
                                if (key == engname and engname == "tensor"
                                        and it["raw"] is None):
                                    continue
                                v = vafter[d]
                                if v > want.get(key, 0):
                                    want[key] = v
                            for key, v in want.items():
                                wait(key, v)
                            if it["raw"] is not None:
                                it["raw"](eng, sems[engname], vafter[i] - incs[i])
                                hwm = {}
                            else:
                                name, a, k = it["call"]
                                inst = getattr(eng, name)(*a, **k)
                                if incs[i]:
                                    inst.then_inc(sems[semkey[i]], incs[i])
                        for e2, v in totals.items():
                            if e2 != engname and v:
                                wait(e2, v)
                        for key, v in lane_totals.items():
                            wait(key, v)
                    return _
                block.sync(mk("sync"))
                block.vector(mk("vector"))
                block.scalar(mk("scalar"))
                block.tensor(mk("tensor"))
                block.gpsimd(mk("gpsimd"))
            res = run_bass_kernel_spmd(self.nc, self.in_maps, core_ids=list(range(8)))
        return res.results


def _ln_stats(p, xtile, mcol, rcol, sq, tmpc, i, epscol):
    st = p.step
    st("vector", lambda e: e.tensor_reduce(mcol[:, i:i + 1], xtile, AX.X, OP.add))
    st("vector", lambda e: e.tensor_scalar_mul(mcol[:, i:i + 1], mcol[:, i:i + 1], 1.0 / D))
    st("scalar", lambda e: e.activation(sq[:], xtile, AF.Square))
    st("vector", lambda e: e.tensor_reduce(tmpc[:, 0:1], sq[:], AX.X, OP.add))
    st("vector", lambda e: e.tensor_scalar_mul(tmpc[:, 0:1], tmpc[:, 0:1], 1.0 / D))
    st("vector", lambda e: e.tensor_tensor(tmpc[:, 1:2], mcol[:, i:i + 1], mcol[:, i:i + 1], OP.mult))
    st("vector", lambda e: e.tensor_tensor(tmpc[:, 0:1], tmpc[:, 0:1], tmpc[:, 1:2], OP.subtract))
    st("scalar", lambda e: e.activation(rcol[:, i:i + 1], tmpc[:, 0:1], AF.Ln, bias=epscol[:, 0:1]))
    st("scalar", lambda e: e.activation(rcol[:, i:i + 1], rcol[:, i:i + 1], AF.Exp, scale=-0.5))


def emit_mamba(p, inp, t_ygl, esA):
    x = np.asarray(inp["L0_attn_x"], np.float32)
    WTi = np.ascontiguousarray(np.asarray(inp["L0_attn_in_proj_w"], np.float32).T)
    convw = np.asarray(inp["L0_attn_conv_w"], np.float32)
    convb = np.asarray(inp["L0_attn_conv_b"], np.float32).reshape(DI, 1)
    XPT = np.ascontiguousarray(np.asarray(inp["L0_attn_x_proj_w"], np.float32).T)
    wdt = np.ascontiguousarray(np.asarray(inp["L0_attn_dt_proj_w"], np.float32))
    bdt = np.asarray(inp["L0_attn_dt_proj_b"], np.float32).reshape(DI, 1)
    alog = np.asarray(inp["L0_attn_A_log"], np.float32)
    Dv = np.asarray(inp["L0_attn_D"], np.float32).reshape(DI, 1)

    xs, wts, xpts, diags, cbs, wdts, bdts, alogs, ds = [], [], [], [], [], [], [], [], []
    ar = np.arange(128)
    for core in range(8):
        b, h = core // 2, core % 2
        own = h * DHALF
        oth = (1 - h) * DHALF
        chan = np.concatenate([np.arange(own, own + DHALF), np.arange(oth, oth + DHALF)])
        xs.append(x[b])
        wts.append(np.concatenate(
            [WTi[:, chan], WTi[:, DI + own: DI + own + DHALF]], axis=1))
        xpts.append(XPT[chan, :])
        cw = convw[chan, 0, :]
        dg = np.zeros((16, 4, 128, 128), np.float32)
        for cc in range(16):
            for k in range(4):
                dg[cc, k, ar, ar] = cw[cc * 128 + ar, k]
        diags.append(dg)
        cbs.append(convb[chan])
        wdts.append(wdt[own:own + DHALF])
        bdts.append(bdt[own:own + DHALF])
        alogs.append(alog[own:own + DHALF])
        ds.append(Dv[own:own + DHALF])

    d_x = p.din("x", xs)
    d_wt = p.din("wt", wts)
    d_xpt = p.din("xpt", xpts)
    d_diag = p.din("diag", diags)
    d_cb = p.din("cb", cbs)
    d_wdt = p.din("wdt", wdts)
    d_bdt = p.din("bdt", bdts)
    d_alog = p.din("alog", alogs)
    d_dv = p.din("dv", ds)
    d_idb = p.din("identb", [np.eye(128, dtype=BF)] * 8)
    t_dgb = p.dtmp("dgb", (16, 4, 128, 128), BF16)

    sb = lambda shape, dt=F32: p.sb(shape, dt, es=esA)
    ps = lambda shape, dt=F32: p.ps(shape, dt, es=esA)

    scratch2 = sb((128, 8 * TCA))
    stage = scratch2[:, 0:1536]
    sq = scratch2[:, 0:1024]
    w_in = [sb((128, 3072), BF16) for _ in range(8)]
    w_xp = sb((128, 16 * 33), BF16)
    w_dgb = [sb((128, 4 * 128), BF16) for _ in range(2)]
    dgstage = sb((128, 128), BF16)
    cb_s = sb((128, 16))
    wdt_s = sb((128, 8)); bdt_s = sb((128, 8))
    a_s = sb((128, 8 * 16)); dv_s = sb((128, 8))
    ones_s = sb((65, 128))
    epscol = sb((128, 1))
    identb = sb((128, 128), BF16)
    xt = scratch2[:, 1024:2048]
    mcol = sb((128, 4)); rcol = sb((128, 4)); tmpc = sb((128, 4))
    lnxt = sb((128, 1024), BF16)
    lnxf = sb((128, 8 * TCA), BF16)
    xis = sb((128, 16 * (TCA + 3)), BF16)
    zs = sb((128, 8 * TCA), BF16)
    xco = sb((128, 8 * TCA))
    xcbf = [sb((128, TCA), BF16) for _ in range(2)]
    xp_s = sb((33, TCA))
    xp_rows = sb((65, 11 * TCA))

    def xprow(r):
        return xp_rows[32 * (r % 3): 32 * (r % 3) + 1,
                       (r // 3) * TCA:(r // 3 + 1) * TCA]

    def onesrow(r):
        return ones_s[32 * (r % 3): 32 * (r % 3) + 1, :]
    dts = sb((128, 8 * TCA))
    dtxdc = sb((128, TCA))
    bigs = sb((128, NG * TCA))
    hstate = sb((128, 8 * 16))
    yscr = sb((128, TCA)); y2scr = sb((128, TCA))
    zgate = sb((128, TCA))
    ygout = sb((128, TCA), BF16)

    pA = ps((128, 512)); pB = ps((128, 512))
    pXP = ps((33, 512))
    pT = ps((128, 128), BF16)
    pR = [ps((128, 512)) for _ in range(2)]
    pC = [ps((128, 512)) for _ in range(2)]

    st = p.step
    dma = p.dma

    dma(identb[:], d_idb)
    stg2 = [scratch2[:, 0:1536], scratch2[:, 1536:3072]]
    for jc in range(8):
        for half in range(2):
            sgi = stg2[(jc * 2 + half) % 2]
            dma(sgi, d_wt[jc * 128:(jc + 1) * 128, half * 1536:(half + 1) * 1536])
            st("vector", (lambda jc=jc, half=half, sgi=sgi: (lambda e: e.tensor_copy(
                w_in[jc][:, half * 1536:(half + 1) * 1536], sgi)))())
    for dc in range(16):
        sgi = scratch2[:, (dc % 2) * 64:(dc % 2) * 64 + 33]
        dma(sgi, d_xpt[dc * 128:(dc + 1) * 128, :])
        st("vector", (lambda dc=dc, sgi=sgi: (lambda e: e.tensor_copy(
            w_xp[:, dc * 33:(dc + 1) * 33], sgi)))())
    for cc in range(16):
        for k in range(4):
            j = (cc * 4 + k) % 2
            sgi = scratch2[:, 256 + j * 128: 256 + (j + 1) * 128]
            dma(sgi, d_diag[cc, k])
            st("vector", (lambda sgi=sgi: (lambda e: e.tensor_copy(
                dgstage[:], sgi)))())
            dma(t_dgb[cc, k], dgstage[:])
    for cc in range(16):
        dma(cb_s[:, cc:cc + 1], d_cb[cc * 128:(cc + 1) * 128, :])
    for dc in range(8):
        dma(wdt_s[:, dc:dc + 1], d_wdt[dc * 128:(dc + 1) * 128, :])
        dma(bdt_s[:, dc:dc + 1], d_bdt[dc * 128:(dc + 1) * 128, :])
        dma(dv_s[:, dc:dc + 1], d_dv[dc * 128:(dc + 1) * 128, :])
        dma(a_s[:, dc * 16:(dc + 1) * 16], d_alog[dc * 128:(dc + 1) * 128, :])
    st("scalar", lambda e: e.activation(a_s[:], a_s[:], AF.Exp))
    st("vector", lambda e: e.tensor_scalar_mul(a_s[:], a_s[:], -1.0))
    st("vector", lambda e: e.memset(ones_s[:], 1.0))
    st("vector", lambda e: e.memset(epscol[:], EPS))
    st("vector", lambda e: e.memset(hstate[:], 0.0))
    st("vector", lambda e: e.memset(xis[:], 0.0))

    for c in range(NCA):
        t0 = c * TCA
        for i in range(NTILE):
            dma(xt, d_x[t0 + i * 128: t0 + (i + 1) * 128, :])
            _ln_stats(p, xt, mcol, rcol, sq, tmpc, i % 4, epscol)
            st("vector", (lambda i=i: (lambda e: e.tensor_scalar(
                lnxt[:], xt, mcol[:, i % 4:i % 4 + 1],
                rcol[:, i % 4:i % 4 + 1], OP.subtract, OP.mult)))())
            for jc in range(8):
                st("tensor", (lambda jc=jc: (lambda e: e.transpose(
                    pT[:], lnxt[:, jc * 128:(jc + 1) * 128], identb[:])))())
                st("scalar", (lambda i=i, jc=jc: (lambda e: e.activation(
                    lnxf[:, jc * TCA + i * 128: jc * TCA + (i + 1) * 128], pT[:],
                    AF.Copy)))())

        for mc in range(24):
            for jc in range(8):
                st("tensor", (lambda mc=mc, jc=jc: (lambda e: e.matmul(
                    pA[:, 0:TCA], w_in[jc][:, mc * 128:(mc + 1) * 128],
                    lnxf[:, jc * TCA:(jc + 1) * TCA],
                    start=(jc == 0), stop=(jc == 7))))())
            if mc < 16:
                o = mc * (TCA + 3) + 3
                st("scalar", (lambda o=o: (lambda e: e.activation(
                    xis[:, o: o + TCA], pA[:, 0:TCA], AF.Copy)))())
            else:
                zc = mc - 16
                st("scalar", (lambda zc=zc: (lambda e: e.activation(
                    zs[:, zc * TCA:(zc + 1) * TCA], pA[:, 0:TCA], AF.Silu)))())

        for cc in range(16):
            base = cc * (TCA + 3)
            dma(w_dgb[cc % 2][:].rearrange("a (k b) -> a k b", b=128),
                t_dgb[cc].rearrange("k a b -> a k b"))
            for k in range(4):
                st("tensor", (lambda cc=cc, k=k, base=base: (lambda e: e.matmul(
                    pB[:, 0:TCA], w_dgb[cc % 2][:, k * 128:(k + 1) * 128],
                    xis[:, base + k: base + k + TCA],
                    start=(k == 0), stop=(k == 3))))())
            st("vector", (lambda base=base: (lambda e: e.tensor_copy(
                xis[:, base: base + 3], xis[:, base + TCA: base + TCA + 3])))())
            st("scalar", (lambda cc=cc: (lambda e: e.activation(
                xcbf[cc % 2][:], pB[:, 0:TCA], AF.Silu, bias=cb_s[:, cc:cc + 1])))())
            if cc < 8:
                st("scalar", (lambda cc=cc: (lambda e: e.activation(
                    xco[:, cc * TCA:(cc + 1) * TCA], pB[:, 0:TCA], AF.Silu,
                    bias=cb_s[:, cc:cc + 1])))())
            st("tensor", (lambda cc=cc: (lambda e: e.matmul(
                pXP[:, 0:TCA], w_xp[:, cc * 33:(cc + 1) * 33], xcbf[cc % 2][:],
                start=(cc == 0), stop=(cc == 15))))())
        st("scalar", lambda e: e.activation(xp_s[:], pXP[:, 0:TCA], AF.Copy))
        for r in range(33):
            dma(xprow(r), xp_s[r:r + 1, :])

        st("tensor", lambda e: e.matmul(pR[0][:, 0:TCA], onesrow(0),
                                        xprow(0), start=True, stop=True))
        for dc in range(8):
            st("scalar", (lambda dc=dc: (lambda e: e.activation(
                dts[:, dc * TCA:(dc + 1) * TCA], pR[0][:, 0:TCA], AF.Identity,
                scale=wdt_s[:, dc:dc + 1], bias=bdt_s[:, dc:dc + 1])))())
        st("scalar", lambda e: e.activation(scratch2[:], dts[:], AF.Abs))
        st("scalar", lambda e: e.activation(scratch2[:], scratch2[:], AF.Exp, scale=-1.0))
        st("scalar", lambda e: e.activation(scratch2[:], scratch2[:], AF.Ln, bias=1.0))
        st("scalar", lambda e: e.activation(dts[:], dts[:], AF.Relu))
        st("vector", lambda e: e.tensor_add(dts[:], dts[:], scratch2[:]))

        for dc in range(8):
            dsl = dts[:, dc * TCA:(dc + 1) * TCA]
            st("vector", (lambda dc=dc: (lambda e: e.tensor_tensor(
                dtxdc[:], dts[:, dc * TCA:(dc + 1) * TCA],
                xco[:, dc * TCA:(dc + 1) * TCA], OP.mult)))())
            st("vector", lambda e: e.memset(yscr[:], 0.0))
            for g in range(2):
                for nl in range(NG):
                    n = g * NG + nl
                    dascr = scratch2[:, (nl % 2) * TCA:(nl % 2 + 1) * TCA]
                    dbscr = scratch2[:, (2 + nl % 2) * TCA:(3 + nl % 2) * TCA]
                    st("tensor", (lambda n=n, nl=nl: (lambda e: e.matmul(
                        pR[nl % 2][:, 0:TCA], onesrow(1 + n),
                        xprow(1 + n), start=True, stop=True)))())
                    st("scalar", (lambda dsl=dsl, dc=dc, n=n, dascr=dascr:
                        (lambda e: e.activation(
                            dascr, dsl, AF.Exp,
                            scale=a_s[:, dc * 16 + n: dc * 16 + n + 1])))())
                    st("vector", (lambda nl=nl, dbscr=dbscr: (lambda e: e.tensor_tensor(
                        dbscr, dtxdc[:], pR[nl % 2][:, 0:TCA], OP.mult)))())
                    st("vector", (lambda dc=dc, n=n, nl=nl, dascr=dascr, dbscr=dbscr:
                        (lambda e: e.tensor_tensor_scan(
                            bigs[:, nl: NG * TCA: NG], dascr, dbscr,
                            hstate[:, dc * 16 + n: dc * 16 + n + 1], OP.mult, OP.add)))())
                st("vector", (lambda dc=dc, g=g: (lambda e: e.tensor_copy(
                    hstate[:, dc * 16 + g * NG: dc * 16 + (g + 1) * NG],
                    bigs[:, (TCA - 1) * NG: TCA * NG])))())
                for nl in range(NG):
                    n = g * NG + nl
                    st("tensor", (lambda n=n, nl=nl: (lambda e: e.matmul(
                        pC[nl % 2][:, 0:TCA], onesrow(17 + n),
                        xprow(17 + n), start=True, stop=True)))())
                    st("vector", (lambda nl=nl: (lambda e: e.tensor_tensor(
                        bigs[:, nl: NG * TCA: NG], bigs[:, nl: NG * TCA: NG],
                        pC[nl % 2][:, 0:TCA], OP.mult)))())
                st("vector", lambda e: e.tensor_reduce(
                    y2scr[:].rearrange("p (t o) -> p t o", o=1),
                    bigs[:].rearrange("p (t n) -> p t n", n=NG), AX.X, OP.add))
                st("gpsimd", lambda e: e.tensor_add(yscr[:], yscr[:], y2scr[:]))
            st("vector", (lambda dc=dc: (lambda e: e.scalar_tensor_tensor(
                yscr[:], xco[:, dc * TCA:(dc + 1) * TCA], dv_s[:, dc:dc + 1],
                yscr[:], OP.mult, OP.add)))())
            st("gpsimd", (lambda dc=dc: (lambda e: e.tensor_copy(
                zgate[:], zs[:, dc * TCA:(dc + 1) * TCA])))())
            st("vector", (lambda dc=dc: (lambda e: e.tensor_tensor(
                ygout[:], yscr[:], zgate[:], OP.mult)))())
            dma(t_ygl[dc * 128:(dc + 1) * 128, t0:t0 + TCA], ygout[:], eng="gpsimd")
    return scratch2


def make_publish(p, t_ygl, t_ygsh, t_fl, d_nc, scratch2, flag1):
    nc = p.nc
    stg = scratch2[:].bitcast(BF16)
    lA = p.es.enter_context(nc.semaphore("pub_lA"))
    lB = p.es.enter_context(nc.semaphore("pub_lB"))

    INC = 2 * 16

    def emitter(eng, sem, base):
        eng.dma_start(out=flag1[:], in_=d_nc).then_inc(sem, 16)
        pid = eng.partition_id()
        with eng.register("rpar1") as rpar:
            eng.reg_alu(rpar, pid, 2, OP.mod)

            def branch(me):
                lanes = [lA, lB]
                cnt = [0, 0]
                for rb in range(8):
                    ln = rb % 2
                    if cnt[ln]:
                        eng.wait_ge(lanes[ln], cnt[ln])
                    eng.dma_start(
                        out=stg[:, ln * 4096:(ln + 1) * 4096],
                        in_=t_ygl[rb * 128:(rb + 1) * 128, :]
                    ).then_inc(lanes[ln], 16)
                    cnt[ln] += 16
                    eng.wait_ge(lanes[ln], cnt[ln])
                    eng.dma_start(
                        out=t_ygsh[me, rb * 128:(rb + 1) * 128, :],
                        in_=stg[:, ln * 4096:(ln + 1) * 4096]
                    ).then_inc(lanes[ln], 16)
                    cnt[ln] += 16
                eng.wait_ge(lanes[0], cnt[0])
                eng.wait_ge(lanes[1], cnt[1])
                eng.wait_ge(sem, base + 16)
                eng.dma_start(out=t_fl[0:1, me * 2:me * 2 + 2],
                              in_=flag1[:]).then_inc(sem, 16)

            with eng.If_eq(rpar, 0):
                branch(0)
            with eng.Else():
                branch(1)
        eng.wait_ge(sem, base + INC)

    return emitter, INC


def make_windows(p, t_ygsh, t_fl, t_ygw, d_nc, zband, bstages):
    nc = p.nc
    flag1b = p.sb((1, 2), I32)
    flagp = p.sb((1, 2), I32)
    wbufs = [bstages[0][:].bitcast(BF16), bstages[1][:].bitcast(BF16)]
    s_poll = p.es.enter_context(nc.semaphore("s_poll"))
    lA = p.es.enter_context(nc.semaphore("win_lA"))
    lB = p.es.enter_context(nc.semaphore("win_lB"))

    INC = 2 * 16
    WV = 2112

    def emitter(eng, sem, base):
        eng.dma_start(out=flag1b[:], in_=d_nc).then_inc(sem, 16)
        eng.wait_ge(sem, base + 16)
        pid = eng.partition_id()
        with eng.register("rpar2") as rpar:
            eng.reg_alu(rpar, pid, 2, OP.mod)

            def branch(me, pe):
                tag = f"w{me}"
                with eng.register(f"rc{tag}") as rc, \
                     eng.register(f"rv{tag}") as rv, \
                     eng.register(f"rw0{tag}") as rw0, \
                     eng.register(f"rw1{tag}") as rw1, \
                     eng.register(f"rn0{tag}") as rn0, \
                     eng.register(f"rn1{tag}") as rn1, \
                     eng.register(f"rc2{tag}") as rc2:
                    eng.load(rn0, flag1b[0:1, 0:1])
                    eng.load(rn1, flag1b[0:1, 1:2])
                    eng.reg_mov(rc, 1)
                    eng.reg_mov(rv, 16)
                    with eng.While(rc):
                        eng.dma_start(out=flagp[:],
                                      in_=t_fl[0:1, pe * 2:pe * 2 + 2]
                                      ).then_inc(s_poll, 16)
                        eng.wait_ge(s_poll, rv)
                        eng.reg_add(rv, rv, 16)
                        eng.load(rw0, flagp[0:1, 0:1])
                        eng.load(rw1, flagp[0:1, 1:2])
                        eng.reg_alu(rc, rw0, rn0, OP.not_equal)
                        eng.reg_alu(rc2, rw1, rn1, OP.not_equal)
                        eng.reg_alu(rc, rc, rc2, OP.logical_or)
                lanes = [lA, lB]
                cnt = [0, 0]
                zlo, zhi = (0, 64) if me == 0 else (2112, 2176)
                for rb in range(16):
                    ln = rb % 2
                    eng.dma_start(out=t_ygw[rb * 128:(rb + 1) * 128, zlo:zhi],
                                  in_=zband[:]).then_inc(lanes[ln], 16)
                    cnt[ln] += 16
                scol = 0 if me == 0 else S - WV
                dcol = 64 if me == 0 else 0
                for s in range(2):
                    for rb in range(8):
                        ln = (s * 8 + rb) % 2
                        eng.wait_ge(lanes[ln], cnt[ln])
                        eng.dma_start(
                            out=wbufs[ln][:, 0:WV],
                            in_=t_ygsh[s, rb * 128:(rb + 1) * 128, scol:scol + WV]
                        ).then_inc(lanes[ln], 16)
                        cnt[ln] += 16
                        eng.wait_ge(lanes[ln], cnt[ln])
                        eng.dma_start(
                            out=t_ygw[s * DHALF + rb * 128: s * DHALF + (rb + 1) * 128,
                                      dcol:dcol + WV],
                            in_=wbufs[ln][:, 0:WV]
                        ).then_inc(lanes[ln], 16)
                        cnt[ln] += 16
                eng.wait_ge(lanes[0], cnt[0])
                eng.wait_ge(lanes[1], cnt[1])
                eng.dma_start(out=t_fl[0:1, 8 + me:9 + me],
                              in_=flag1b[0:1, 0:1]).then_inc(sem, 16)

            with eng.If_eq(rpar, 0):
                branch(0, 1)
            with eng.Else():
                branch(1, 0)
        eng.wait_ge(sem, base + INC)

    return emitter, INC


def emit_ffn_weights(p, inp, esB):
    x = np.asarray(inp["L0_attn_x"], np.float32)
    WOT = np.ascontiguousarray(np.asarray(inp["L0_attn_out_proj_w"], np.float32).T)
    W1T = np.ascontiguousarray(np.asarray(inp["L0_ffn_w1"], np.float32).T)
    W2T = np.ascontiguousarray(np.asarray(inp["L0_ffn_w2"], np.float32).T)
    fcw = np.asarray(inp["L0_ffn_conv_w"], np.float32)
    fcb = np.asarray(inp["L0_ffn_conv_b"], np.float32).reshape(DH, 1)

    fdiag = np.zeros((32, 3, 128, 128), np.float32)
    ar = np.arange(128)
    for m in range(32):
        for k in range(3):
            fdiag[m, k, ar, ar] = fcw[m * 128 + ar, 0, k]

    xps = []
    for core in range(8):
        b, th = core // 2, core % 2
        t0 = th * THALF - PAD
        xp = np.zeros((TB, D), np.float32)
        lo, hi = max(0, t0), min(S, t0 + TB)
        xp[lo - t0:hi - t0, :] = x[b][lo:hi, :]
        xps.append(xp)

    d_xw = p.din("xw", xps)
    d_wot = p.din("wot", [WOT] * 8)
    d_w1t = p.din("w1t", [W1T] * 8)
    d_w2t = p.din("w2t", [W2T] * 8)
    d_fdg = p.din("fdg", [fdiag] * 8)
    d_fcb = p.din("fcb", [fcb] * 8)
    d_idb2 = p.din("identb2", [np.eye(128, dtype=BF)] * 8)
    d_idf = p.din("identf", [np.eye(128, dtype=np.float32)] * 8)

    o_out = p.dout("out", (THALF, D))

    t_x2 = p.dtmp("x2s", (TB, D))
    t_xq1 = p.dtmp("xq1s", (8, 128, TB), BF16)
    t_s1i = p.dtmp("s1is", (TB, 1))
    t_s2 = p.dtmp("s2s", (THALF, 1)); t_s2i = p.dtmp("s2is", (THALF, 1))
    t_wq1 = p.dtmp("wq1s", (32, 8, 128, 128), BF16)
    t_wq2 = p.dtmp("wq2s", (8, 32, 128, 128), BF16)
    t_wot = p.dtmp("wots", (8, 16, 128, 128), BF16)
    t_fdg = p.dtmp("fdgs", (32, 3, 128, 128), BF16)

    sb = lambda shape, dt=F32: p.sb(shape, dt, es=esB)
    ps = lambda shape, dt=F32: p.ps(shape, dt, es=esB)

    stage = sb((128, 2048))
    stage2 = sb((128, 2048))
    qi32 = sb((128, 2048), I32)
    qbf = sb((128, TB), BF16)
    acc8 = sb((128, 34))
    onescol = sb((128, 1))
    ones_s = sb((1, 128))
    epscol = sb((128, 1))
    mw1col = sb((128, 1))
    mw2col = sb((128, 1))
    minv = sb((128, 1))
    identb = sb((128, 128), BF16)
    identf = sb((128, 128))
    fcb_s = sb((128, 32))
    ygbf = sb((128, 16 * 512), BF16)
    wotb = [sb((128, 16 * 128), BF16) for _ in range(2)]
    afm = sb((128, 512))
    xt1 = sb((128, 1024))
    x2buf = sb((128, 4 * 1024))
    mcol = sb((128, 2)); rcol = sb((128, 2)); tmpc = sb((128, 2))
    sq = sb((128, 1024))
    u_scr = sb((128, 1024))
    amax = sb((128, 1)); s1col = sb((128, 1)); rscol = sb((128, 1))
    s1icol = sb((128, 1))
    xq1i = sb((128, 1024), I32)
    xq1t = sb((128, 1024), BF16)
    s1irow = sb((1, TW)); arow = sb((1, TW))
    arep = sb((128, TW))
    xq1b = sb((128, 8 * TW), BF16)
    w1b = [sb((128, 8 * 128), BF16) for _ in range(2)]
    h1s = sb((128, 32 * TW), BF16)
    fdgb = [sb((128, 3 * 128), BF16) for _ in range(2)]
    h2 = sb((128, 32 * TCB))
    macc = sb((128, TCB))
    habs = sb((128, TCB))
    mts = sb((128, 128))
    mxc = sb((128, 2)); s2col = sb((128, 2)); s2icol = sb((128, 2))
    s2row = sb((1, TCB)); brow = sb((1, TCB))
    g2rep = sb((128, TCB)); brep = sb((128, TCB))
    xq2i = sb((128, TCB), I32)
    xq2 = sb((128, 32 * TCB), BF16)
    w2b = [sb((128, 32 * 128), BF16) for _ in range(2)]
    ffm = sb((128, TCB))
    x2c = sb((128, 2 * 1024))
    ot = sb((128, 2 * 1024))

    pA = ps((128, 512)); pB = ps((128, 512))
    pCv = ps((128, 512))
    pT32 = ps((128, 128)); pTb = ps((128, 128), BF16)
    pR = ps((128, 512)); pS = ps((128, 8))

    st = p.step
    dma = p.dma

    dma(identb[:], d_idb2)
    dma(identf[:], d_idf)
    st("vector", lambda e: e.memset(ones_s[:], 1.0))
    st("vector", lambda e: e.memset(onescol[:], 1.0))
    st("vector", lambda e: e.memset(epscol[:], EPS))
    for m in range(32):
        dma(fcb_s[:, m:m + 1], d_fcb[m * 128:(m + 1) * 128, :])

    def wquant(d_wt, ntiles, width, mcol_, t_scr, nval):
        nh = (width + 2047) // 2048
        hw = width // nh
        for i in range(ntiles):
            for q in range(nh):
                sgi = (stage if (i * nh + q) % 2 == 0 else stage2)[:, 0:hw]
                dma(sgi, d_wt[i * 128:(i + 1) * 128, q * hw:(q + 1) * hw])
                st("vector", (lambda i=i, q=q, hw=hw, nh=nh, sgi=sgi:
                    (lambda e: e.tensor_reduce(
                        acc8[:, i * nh + q: i * nh + q + 1], sgi, AX.X,
                        OP.add, apply_absolute_value=True)))())
        ncols = ntiles * nh
        st("vector", (lambda ncols=ncols: (lambda e: e.tensor_reduce(
            acc8[:, 33:34].rearrange("p (o q) -> p o q", q=1),
            acc8[:, 0:ncols].rearrange("p (o t) -> p o t", o=1), AX.X, OP.add)))())
        st("tensor", lambda e: e.matmul(pS[0:1, 0:1], onescol[:], acc8[:, 33:34],
                                        start=True, stop=True))
        st("scalar", lambda e: e.activation(acc8[0:1, 32:33], pS[0:1, 0:1], AF.Copy))
        st("vector", (lambda nval=nval: (lambda e: e.tensor_scalar_mul(
            acc8[0:1, 32:33], acc8[0:1, 32:33], 1.0 / nval)))())
        st("vector", lambda e: e.tensor_scalar_max(acc8[0:1, 32:33], acc8[0:1, 32:33], 1e-5))
        st("tensor", lambda e: e.matmul(pS[:, 1:2], ones_s[0:1, :], acc8[0:1, 32:33],
                                        start=True, stop=True))
        st("scalar", (lambda mcol_=mcol_: (lambda e: e.activation(
            mcol_[:], pS[:, 1:2], AF.Copy)))())
        st("vector", (lambda mcol_=mcol_: (lambda e: e.reciprocal(minv[:], mcol_[:])))())
        for i in range(ntiles):
            for q in range(nh):
                sgi = (stage if (i * nh + q) % 2 == 0 else stage2)[:, 0:hw]
                dma(sgi, d_wt[i * 128:(i + 1) * 128, q * hw:(q + 1) * hw])
                st("vector", (lambda hw=hw, sgi=sgi: (lambda e: e.tensor_scalar(
                    sgi, sgi, minv[:, 0:1], 1.49, OP.mult, OP.min)))())
                st("vector", (lambda hw=hw, sgi=sgi: (lambda e: e.tensor_scalar(
                    qi32[:, 0:hw], sgi, -1.49, None, OP.max)))())
                st("vector", (lambda hw=hw: (lambda e: e.tensor_copy(
                    qbf[:, 0:hw], qi32[:, 0:hw])))())
                mph = hw // 128
                dma(t_scr[q * mph:(q + 1) * mph, i].rearrange("m j h -> j m h"),
                    qbf[:, 0:hw].rearrange("j (m h) -> j m h", h=128))
    wquant(d_w1t, 8, 4096, mw1col, t_wq1, DH * D)
    wquant(d_w2t, 32, 1024, mw2col, t_wq2, DH * D)
    for i in range(16):
        dma(stage[:, 0:1024], d_wot[i * 128:(i + 1) * 128, :])
        st("vector", lambda e: e.tensor_copy(qbf[:, 0:1024], stage[:, 0:1024]))
        dma(t_wot[:, i].rearrange("m d c -> d m c"),
            qbf[:, 0:1024].rearrange("d (m c) -> d m c", c=128))
    for m in range(32):
        for k in range(3):
            dma(stage[:, 0:128], d_fdg[m, k])
            st("vector", lambda e: e.tensor_copy(qbf[:, 0:128], stage[:, 0:128]))
            dma(t_fdg[m, k], qbf[:, 0:128])

    return dict(locals())


def emit_ffn_main(p, ctx, t_ygw):
    st = ctx['st']
    dma = ctx['dma']
    d_xw = ctx['d_xw']
    o_out = ctx['o_out']
    t_x2 = ctx['t_x2']
    t_xq1 = ctx['t_xq1']
    t_s1i = ctx['t_s1i']
    t_s2 = ctx['t_s2']
    t_s2i = ctx['t_s2i']
    t_wq1 = ctx['t_wq1']
    t_wq2 = ctx['t_wq2']
    t_wot = ctx['t_wot']
    t_fdg = ctx['t_fdg']
    stage = ctx['stage']
    qi32 = ctx['qi32']
    qbf = ctx['qbf']
    acc8 = ctx['acc8']
    onescol = ctx['onescol']
    ones_s = ctx['ones_s']
    epscol = ctx['epscol']
    mw1col = ctx['mw1col']
    mw2col = ctx['mw2col']
    minv = ctx['minv']
    identb = ctx['identb']
    identf = ctx['identf']
    fcb_s = ctx['fcb_s']
    ygbf = ctx['ygbf']
    wotb = ctx['wotb']
    afm = ctx['afm']
    xt1 = ctx['xt1']
    x2buf = ctx['x2buf']
    mcol = ctx['mcol']
    rcol = ctx['rcol']
    tmpc = ctx['tmpc']
    sq = ctx['sq']
    u_scr = ctx['u_scr']
    amax = ctx['amax']
    s1col = ctx['s1col']
    rscol = ctx['rscol']
    s1icol = ctx['s1icol']
    xq1i = ctx['xq1i']
    xq1t = ctx['xq1t']
    s1irow = ctx['s1irow']
    arow = ctx['arow']
    arep = ctx['arep']
    xq1b = ctx['xq1b']
    w1b = ctx['w1b']
    h1s = ctx['h1s']
    fdgb = ctx['fdgb']
    h2 = ctx['h2']
    macc = ctx['macc']
    habs = ctx['habs']
    mts = ctx['mts']
    mxc = ctx['mxc']
    s2col = ctx['s2col']
    s2icol = ctx['s2icol']
    s2row = ctx['s2row']
    brow = ctx['brow']
    g2rep = ctx['g2rep']
    brep = ctx['brep']
    xq2i = ctx['xq2i']
    xq2 = ctx['xq2']
    w2b = ctx['w2b']
    ffm = ctx['ffm']
    x2c = ctx['x2c']
    ot = ctx['ot']
    pA = ctx['pA']
    pB = ctx['pB']
    pT32 = ctx['pT32']
    pTb = ctx['pTb']
    pR = ctx['pR']
    pS = ctx['pS']
    pCv = ctx['pCv']
    stage2 = ctx['stage2']
    fcs = [(0, 512), (512, 512), (1024, 512), (1536, 512), (2048, 128)]
    for f0, fc in fcs:
        for dc in range(16):
            dma(ygbf[:, dc * 512: dc * 512 + fc],
                t_ygw[dc * 128:(dc + 1) * 128, f0:f0 + fc])
        nt = fc // 128
        for dm in range(8):
            wob = wotb[dm % 2]
            pAc = pA if dm % 2 == 0 else pB
            dma(wob[:].rearrange("d (dc c) -> d dc c", c=128),
                t_wot[dm].rearrange("dc d c -> d dc c"))
            for dc in range(16):
                st("tensor", (lambda dm=dm, dc=dc, fc=fc, wob=wob, pAc=pAc:
                    (lambda e: e.matmul(
                        pAc[:, 0:fc], wob[:, dc * 128:(dc + 1) * 128],
                        ygbf[:, dc * 512: dc * 512 + fc],
                        start=(dc == 0), stop=(dc == 15))))())
            st("scalar", (lambda fc=fc, pAc=pAc: (lambda e: e.activation(
                afm[:, 0:fc], pAc[:, 0:fc], AF.Copy)))())
            for ts in range(nt):
                st("tensor", (lambda ts=ts: (lambda e: e.transpose(
                    pT32[:], afm[:, ts * 128:(ts + 1) * 128], identf[:])))())
                st("vector", (lambda dm=dm, ts=ts: (lambda e: e.tensor_copy(
                    x2buf[:, ts * 1024 + dm * 128: ts * 1024 + (dm + 1) * 128],
                    pT32[:])))())
        for ts in range(nt):
            ti = f0 // 128 + ts
            xb = x2buf[:, ts * 1024:(ts + 1) * 1024]
            dma(xt1[:], d_xw[ti * 128:(ti + 1) * 128, :])
            st("vector", (lambda xb=xb: (lambda e: e.tensor_tensor(
                xb, xb, xt1[:], OP.add)))())
            dma(t_x2[ti * 128:(ti + 1) * 128, :], xb, eng="gpsimd")
            _ln_stats(p, xb, mcol, rcol, sq, tmpc, 0, epscol)
            st("vector", (lambda xb=xb: (lambda e: e.tensor_scalar(
                u_scr[:], xb, mcol[:, 0:1], None, OP.subtract)))())
            st("vector", lambda e: e.tensor_reduce(
                amax[:], u_scr[:], AX.X, OP.max, apply_absolute_value=True))
            st("vector", lambda e: e.tensor_tensor(amax[:], amax[:], rcol[:, 0:1], OP.mult))
            st("vector", lambda e: e.tensor_scalar_max(amax[:], amax[:], 1e-5))
            st("vector", lambda e: e.reciprocal(s1col[:], amax[:]))
            st("vector", lambda e: e.tensor_scalar_mul(s1col[:], s1col[:], 127.0))
            st("vector", lambda e: e.tensor_tensor(rscol[:], rcol[:, 0:1], s1col[:], OP.mult))
            st("vector", lambda e: e.tensor_scalar_mul(s1icol[:], amax[:], 1.0 / 127.0))
            dma(t_s1i[ti * 128:(ti + 1) * 128, :], s1icol[:], eng="gpsimd")
            st("vector", (lambda xb=xb: (lambda e: e.tensor_scalar(
                xq1i[:], xb, mcol[:, 0:1], rscol[:, 0:1], OP.subtract, OP.mult)))())
            st("vector", lambda e: e.tensor_copy(xq1t[:], xq1i[:]))
            for jc in range(8):
                st("tensor", (lambda jc=jc: (lambda e: e.transpose(
                    pTb[:], xq1t[:, jc * 128:(jc + 1) * 128], identb[:])))())
                st("scalar", (lambda jc=jc: (lambda e: e.activation(
                    qbf[:, 0:128], pTb[:], AF.Copy)))())
                dma(t_xq1[jc, :, ti * 128:(ti + 1) * 128], qbf[:, 0:128], eng="gpsimd")

    for c in range(NCB):
        q0 = PAD + c * TCB - 1
        dma(s1irow[:], t_s1i.rearrange("t o -> o t")[:, q0:q0 + TW])
        st("vector", lambda e: e.tensor_scalar(
            arow[:], s1irow[:], mw1col[0:1, 0:1], None, OP.mult))
        st("tensor", lambda e: e.matmul(pR[:, 0:TW], ones_s[0:1, :], arow[:],
                                        start=True, stop=True))
        st("scalar", lambda e: e.activation(arep[:], pR[:, 0:TW], AF.Copy))
        for jc in range(8):
            dma(xq1b[:, jc * TW:(jc + 1) * TW], t_xq1[jc, :, q0:q0 + TW])
        for m in range(32):
            pH = pA if m % 2 == 0 else pB
            dma(w1b[m % 2][:].rearrange("j (jc h) -> j jc h", h=128),
                t_wq1[m].rearrange("jc j h -> j jc h"))
            for jc in range(8):
                st("tensor", (lambda m=m, jc=jc, pH=pH: (lambda e: e.matmul(
                    pH[:, 0:TW], w1b[m % 2][:, jc * 128:(jc + 1) * 128],
                    xq1b[:, jc * TW:(jc + 1) * TW],
                    start=(jc == 0), stop=(jc == 7))))())
            st("vector", (lambda m=m, pH=pH: (lambda e: e.tensor_tensor(
                h1s[:, m * TW:(m + 1) * TW], pH[:, 0:TW], arep[:], OP.mult)))())
        st("vector", lambda e: e.memset(macc[:], 0.0))
        for m in range(32):
            dma(fdgb[m % 2][:].rearrange("a (k b) -> a k b", b=128),
                t_fdg[m].rearrange("k a b -> a k b"))
            for k in range(3):
                st("tensor", (lambda m=m, k=k: (lambda e: e.matmul(
                    pCv[:, 0:TCB], fdgb[m % 2][:, k * 128:(k + 1) * 128],
                    h1s[:, m * TW + k: m * TW + k + TCB],
                    start=(k == 0), stop=(k == 2))))())
            st("scalar", (lambda m=m: (lambda e: e.activation(
                h2[:, m * TCB:(m + 1) * TCB], pCv[:, 0:TCB], AF.Silu,
                bias=fcb_s[:, m:m + 1])))())
            st("scalar", (lambda m=m: (lambda e: e.activation(
                habs[:], h2[:, m * TCB:(m + 1) * TCB], AF.Abs)))())
            st("vector", lambda e: e.tensor_tensor(macc[:], macc[:], habs[:], OP.max))
        for ts in range(TCB // 128):
            st("tensor", (lambda ts=ts: (lambda e: e.transpose(
                pT32[:], macc[:, ts * 128:(ts + 1) * 128], identf[:])))())
            st("vector", lambda e: e.tensor_copy(mts[:], pT32[:]))
            st("vector", (lambda ts=ts: (lambda e: e.tensor_reduce(
                mxc[:, ts:ts + 1], mts[:], AX.X, OP.max)))())
            st("vector", (lambda ts=ts: (lambda e: e.tensor_scalar_max(
                mxc[:, ts:ts + 1], mxc[:, ts:ts + 1], 1e-5)))())
            st("vector", (lambda ts=ts: (lambda e: e.reciprocal(
                s2col[:, ts:ts + 1], mxc[:, ts:ts + 1])))())
            st("vector", (lambda ts=ts: (lambda e: e.tensor_scalar_mul(
                s2col[:, ts:ts + 1], s2col[:, ts:ts + 1], 127.0)))())
            st("vector", (lambda ts=ts: (lambda e: e.tensor_scalar_mul(
                s2icol[:, ts:ts + 1], mxc[:, ts:ts + 1], 1.0 / 127.0)))())
            dma(t_s2[c * TCB + ts * 128: c * TCB + (ts + 1) * 128, :],
                s2col[:, ts:ts + 1], eng="gpsimd")
            dma(t_s2i[c * TCB + ts * 128: c * TCB + (ts + 1) * 128, :],
                s2icol[:, ts:ts + 1], eng="gpsimd")
        dma(s2row[:], t_s2.rearrange("t o -> o t")[:, c * TCB:(c + 1) * TCB])
        dma(brow[:], t_s2i.rearrange("t o -> o t")[:, c * TCB:(c + 1) * TCB])
        st("vector", lambda e: e.tensor_scalar(
            brow[:], brow[:], mw2col[0:1, 0:1], None, OP.mult))
        st("tensor", lambda e: e.matmul(pR[:, 0:TCB], ones_s[0:1, :], s2row[:],
                                        start=True, stop=True))
        st("scalar", lambda e: e.activation(g2rep[:], pR[:, 0:TCB], AF.Copy))
        st("tensor", lambda e: e.matmul(pR[:, 0:TCB], ones_s[0:1, :], brow[:],
                                        start=True, stop=True))
        st("scalar", lambda e: e.activation(brep[:], pR[:, 0:TCB], AF.Copy))
        for m in range(32):
            st("vector", (lambda m=m: (lambda e: e.tensor_tensor(
                xq2i[:], h2[:, m * TCB:(m + 1) * TCB], g2rep[:], OP.mult)))())
            st("vector", (lambda m=m: (lambda e: e.tensor_copy(
                xq2[:, m * TCB:(m + 1) * TCB], xq2i[:])))())
        dma(x2c[:, 0:1024], t_x2[q0 + 1: q0 + 1 + 128, :])
        dma(x2c[:, 1024:2048], t_x2[q0 + 1 + 128: q0 + 1 + 256, :])
        for dm in range(8):
            pH2 = pA if dm % 2 == 0 else pB
            dma(w2b[dm % 2][:].rearrange("h (hc c) -> h hc c", c=128),
                t_wq2[dm].rearrange("hc h c -> h hc c"))
            for hc in range(32):
                st("tensor", (lambda dm=dm, hc=hc, pH2=pH2: (lambda e: e.matmul(
                    pH2[:, 0:TCB], w2b[dm % 2][:, hc * 128:(hc + 1) * 128],
                    xq2[:, hc * TCB:(hc + 1) * TCB],
                    start=(hc == 0), stop=(hc == 31))))())
            st("vector", (lambda pH2=pH2: (lambda e: e.tensor_tensor(
                ffm[:], pH2[:, 0:TCB], brep[:], OP.mult)))())
            for ts in range(TCB // 128):
                st("tensor", (lambda dm=dm, ts=ts: (lambda e: e.transpose(
                    pT32[:], ffm[:, ts * 128:(ts + 1) * 128], identf[:])))())
                st("vector", (lambda dm=dm, ts=ts: (lambda e: e.tensor_tensor(
                    ot[:, ts * 1024 + dm * 128: ts * 1024 + (dm + 1) * 128],
                    pT32[:], x2c[:, ts * 1024 + dm * 128: ts * 1024 + (dm + 1) * 128],
                    OP.add)))())
        dma(o_out[c * TCB: c * TCB + 128, :], ot[:, 0:1024], eng="gpsimd")
        dma(o_out[c * TCB + 128: c * TCB + 256, :], ot[:, 1024:2048], eng="gpsimd")


def kernel(**inputs):
    global _dbg
    _dbg = {}
    p = Prog("F")
    t_ygl = p.dtmp("ygl", (DHALF, S), BF16)
    t_ygsh = p.dtmp("ygsh", (2, DHALF, S), BF16, shared=True)
    t_fl = p.dtmp("flg", (1, 16), I32, shared=True)
    t_ygw = p.dtmp("ygw", (DI, TB), BF16)
    rngv = np.random.default_rng()
    n0, n1 = (int(rngv.integers(1, 2**31 - 1)), int(rngv.integers(1, 2**31 - 1)))
    d_nc = p.din("nonce", [np.array([[n0, n1]], np.int32)] * 8)

    flag1 = p.sb((1, 2), I32)
    esA = contextlib.ExitStack()
    scratch2 = emit_mamba(p, inputs, t_ygl, esA)
    pub, INC1 = make_publish(p, t_ygl, t_ygsh, t_fl, d_nc, scratch2, flag1)
    p.raw("sync", pub, INC1)
    esA.close()
    zband = p.sb((128, 64), BF16)
    p.step("vector", lambda e: e.memset(zband[:], 0.0))
    ctx = emit_ffn_weights(p, inputs, p.es)
    win, INC2 = make_windows(p, t_ygsh, t_fl, t_ygw, d_nc, zband,
                             (ctx["stage"], ctx["stage2"]))
    p.raw("sync", win, INC2)
    emit_ffn_main(p, ctx, t_ygw)
    res = p.run()
    if _DEBUG:
        _dbg["F"] = res
    out = np.zeros((B, S, D), np.float32)
    for core in range(8):
        b, th = core // 2, core % 2
        out[b, th * THALF:(th + 1) * THALF, :] = np.asarray(res[core]["out"], np.float32)
    return out


# version 29 (final)
# speedup vs baseline: 1.0137x; 1.0137x over previous
import contextlib
import numpy as np
import ml_dtypes
import concourse.bass as bass
import concourse.mybir as mybir
from concourse.bass_utils import run_bass_kernel_spmd

F32 = mybir.dt.float32
BF16 = mybir.dt.bfloat16
I32 = mybir.dt.int32
OP = mybir.AluOpType
AF = mybir.ActivationFunctionType
AX = mybir.AxisListType
BF = ml_dtypes.bfloat16

B, S, D, DI, DH, NS = 4, 4096, 1024, 2048, 4096, 16
DHALF = DI // 2
THALF = S // 2
TCA = 512
NCA = S // TCA
NTILE = TCA // 128
NG = 8
PAD = 64
TB = THALF + 2 * PAD
SP = S + 2 * PAD
TCB = 256
NCB = THALF // TCB
TW = TCB + 2
EPS = 1e-5

_DEBUG = False
_STUB_EXCHANGE = False
_dbg = {}


class _Rec:
    def __init__(self):
        self.calls = []

    def __getattr__(self, name):
        def call(*a, **k):
            self.calls.append((name, a, k))
            return self
        return call

    def then_inc(self, *a, **k):
        return self


def _is_ap(v):
    return isinstance(v, bass.AP)


class Prog:

    ENGS = ("sync", "vector", "scalar", "tensor", "gpsimd")

    def __init__(self, name):
        self.nc = bass.Bass("TRN2", target_bir_lowering=False, num_devices=8,
                            detect_race_conditions=False)
        self.items = []
        self.hist = {}
        self.in_maps = [dict() for _ in range(8)]
        self.es = contextlib.ExitStack()
        self._ctr = 0
        self.name = name
        self._rowlen = {}
        self._fence = None

    def din(self, name, arrays):
        a0 = np.ascontiguousarray(arrays[0])
        dt = {np.dtype(np.float32): F32, np.dtype(BF): BF16,
              np.dtype(np.int32): I32}[a0.dtype]
        for i in range(8):
            self.in_maps[i][name] = np.ascontiguousarray(arrays[i])
        return self.nc.dram_tensor(name, list(a0.shape), dt, kind="ExternalInput").ap()

    def dout(self, name, shape, dt=F32):
        return self.nc.dram_tensor(name, list(shape), dt, kind="ExternalOutput").ap()

    def dtmp(self, name, shape, dt=F32, shared=False):
        return self.nc.dram_tensor(
            name, list(shape), dt, kind="Internal",
            addr_space="Shared" if shared else "Local").ap()

    def sb(self, shape, dt=F32, es=None):
        self._ctr += 1
        return (es or self.es).enter_context(
            self.nc.sbuf_tensor(f"sb{self._ctr}", shape, dt))

    def ps(self, shape, dt=F32, es=None):
        self._ctr += 1
        return (es or self.es).enter_context(
            self.nc.psum_tensor(f"ps{self._ctr}", shape, dt))

    def _span(self, ap):
        t = ap.tensor
        name = t.name
        dims = ap.ap
        off = ap.offset
        sp = str(ap.space)
        if "SBUF" in sp or "PSUM" in sp:
            rl = self._rowlen.get(name)
            if rl is None:
                shp = list(t.shape)
                rl = 1
                for s in shp[1:]:
                    rl *= s
                self._rowlen[name] = rl
            p0 = off // rl if rl else 0
            f0 = off - p0 * rl
            pc = dims[0][1] if dims else 1
            f1 = f0 + sum(st * (ct - 1) for st, ct in dims[1:]) + 1
            return (name, p0, p0 + pc, f0, f1)
        f1 = off + sum(st * (ct - 1) for st, ct in dims) + 1
        return (name, 0, 128, off, f1)

    @staticmethod
    def _overlap(a, b):
        return a[1] < b[2] and b[1] < a[2] and a[3] < b[4] and b[3] < a[4]

    @staticmethod
    def _contains(outer, inner):
        return (outer[1] <= inner[1] and outer[2] >= inner[2]
                and outer[3] <= inner[3] and outer[4] >= inner[4])

    def _classify(self, name, a, k):
        if name == "dma_start":
            out = k.get("out", a[0] if a else None)
            in_ = k.get("in_", a[1] if len(a) > 1 else None)
            return [out], ([in_] if _is_ap(in_) else [])
        pos = [v for v in a if _is_ap(v)]
        writes = pos[:1]
        reads = pos[1:]
        for kk, v in k.items():
            if not _is_ap(v):
                continue
            if kk == "accum_out":
                writes.append(v)
            else:
                reads.append(v)
        return writes, reads

    def step(self, engine, fn, inc=1):
        rec = _Rec()
        fn(rec)
        assert len(rec.calls) == 1, f"step recorded {len(rec.calls)} calls"
        name, a, k = rec.calls[0]
        writes, reads = self._classify(name, a, k)
        idx = len(self.items)
        deps = set()
        if self._fence is not None:
            deps.add(self._fence)
        for ap in reads:
            s = self._span(ap)
            for ent in reversed(self.hist.get(s[0], ())):
                if ent[1] and self._overlap(s, ent[2]):
                    deps.add(ent[0])
                    if self._contains(ent[2], s):
                        break
        for ap in writes:
            s = self._span(ap)
            for ent in reversed(self.hist.get(s[0], ())):
                if self._overlap(s, ent[2]):
                    deps.add(ent[0])
                    if ent[1] and self._contains(ent[2], s):
                        break
        for ap in reads:
            s = self._span(ap)
            self.hist.setdefault(s[0], []).append((idx, False, s))
        for ap in writes:
            s = self._span(ap)
            self.hist.setdefault(s[0], []).append((idx, True, s))
        self.items.append(dict(engine=engine, call=(name, a, k), inc=inc,
                               deps=deps, raw=None, is_dma=(name == "dma_start")))

    def raw(self, engine, emitter, inc):
        idx = len(self.items)
        deps = set(range(idx))
        self.items.append(dict(engine=engine, call=None, inc=inc, deps=deps,
                               raw=emitter, is_dma=False))
        self._fence = idx

    def dma(self, dst, src, eng="sync"):
        self.step(eng, (lambda dst=dst, src=src:
                        (lambda e: e.dma_start(out=dst, in_=src)))(), 16)

    def run(self):
        nc = self.nc
        NLANES = 6
        lane_count = {}
        lane_prev = {}
        eng_cum = {e: 0 for e in self.ENGS}
        qpos = {}
        n = len(self.items)
        semkey = [None] * n
        vafter = [0] * n
        selfdep = [None] * n
        incs = [0] * n
        needed = [False] * n
        for it in self.items:
            for d in it["deps"]:
                needed[d] = True
        last_per_eng = {}
        for i, it in enumerate(self.items):
            last_per_eng[it["engine"]] = i
        for i in last_per_eng.values():
            needed[i] = True
        for i, it in enumerate(self.items):
            if it["raw"] is not None or it["is_dma"]:
                needed[i] = True
        for i, it in enumerate(self.items):
            e = it["engine"]
            if it["raw"] is not None:
                incs[i] = it["inc"]
                eng_cum[e] += incs[i]
                semkey[i] = e
                vafter[i] = eng_cum[e]
            elif it["is_dma"]:
                j = qpos.get(e, 0) % NLANES
                qpos[e] = qpos.get(e, 0) + 1
                key = (e, j)
                selfdep[i] = lane_prev.get(key)
                lane_prev[key] = i
                lane_count[key] = lane_count.get(key, 0) + 16
                semkey[i] = key
                incs[i] = 16
                vafter[i] = lane_count[key]
            else:
                inc = it["inc"] if needed[i] else 0
                incs[i] = inc
                eng_cum[e] += inc
                semkey[i] = e
                vafter[i] = eng_cum[e]
        totals = dict(eng_cum)
        lane_totals = dict(lane_count)
        with contextlib.ExitStack() as semstack:
            sems = {e: semstack.enter_context(nc.semaphore(f"sem_{e}"))
                    for e in self.ENGS}
            for key in lane_totals:
                sems[key] = semstack.enter_context(
                    nc.semaphore(f"sem_{key[0]}_l{key[1]}"))
            with nc.Block() as block:
                def mk(engname):
                    def _(eng):
                        hwm = {}

                        def wait(key, v):
                            if hwm.get(key, 0) >= v:
                                return
                            eng.wait_ge(sems[key], v)
                            hwm[key] = v
                        for i, it in enumerate(self.items):
                            if it["engine"] != engname:
                                continue
                            want = {}
                            deps = it["deps"]
                            if selfdep[i] is not None:
                                deps = deps | {selfdep[i]}
                            for d in deps:
                                key = semkey[d]
                                if (key == engname and engname == "tensor"
                                        and it["raw"] is None):
                                    continue
                                v = vafter[d]
                                if v > want.get(key, 0):
                                    want[key] = v
                            for key, v in want.items():
                                wait(key, v)
                            if it["raw"] is not None:
                                it["raw"](eng, sems[engname], vafter[i] - incs[i])
                                hwm = {}
                            else:
                                name, a, k = it["call"]
                                inst = getattr(eng, name)(*a, **k)
                                if incs[i]:
                                    inst.then_inc(sems[semkey[i]], incs[i])
                        for e2, v in totals.items():
                            if e2 != engname and v:
                                wait(e2, v)
                        for key, v in lane_totals.items():
                            wait(key, v)
                    return _
                block.sync(mk("sync"))
                block.vector(mk("vector"))
                block.scalar(mk("scalar"))
                block.tensor(mk("tensor"))
                block.gpsimd(mk("gpsimd"))
            res = run_bass_kernel_spmd(self.nc, self.in_maps, core_ids=list(range(8)))
        return res.results


def _ln_stats(p, xtile, mcol, rcol, sq, tmpc, i, epscol):
    st = p.step
    st("vector", lambda e: e.tensor_reduce(mcol[:, i:i + 1], xtile, AX.X, OP.add))
    st("vector", lambda e: e.tensor_scalar_mul(mcol[:, i:i + 1], mcol[:, i:i + 1], 1.0 / D))
    st("scalar", lambda e: e.activation(sq[:], xtile, AF.Square))
    st("vector", lambda e: e.tensor_reduce(tmpc[:, 0:1], sq[:], AX.X, OP.add))
    st("vector", lambda e: e.tensor_scalar_mul(tmpc[:, 0:1], tmpc[:, 0:1], 1.0 / D))
    st("vector", lambda e: e.tensor_tensor(tmpc[:, 1:2], mcol[:, i:i + 1], mcol[:, i:i + 1], OP.mult))
    st("vector", lambda e: e.tensor_tensor(tmpc[:, 0:1], tmpc[:, 0:1], tmpc[:, 1:2], OP.subtract))
    st("scalar", lambda e: e.activation(rcol[:, i:i + 1], tmpc[:, 0:1], AF.Ln, bias=epscol[:, 0:1]))
    st("scalar", lambda e: e.activation(rcol[:, i:i + 1], rcol[:, i:i + 1], AF.Exp, scale=-0.5))


def emit_mamba(p, inp, t_ygl, esA):
    x = np.asarray(inp["L0_attn_x"], np.float32)
    WTi = np.ascontiguousarray(np.asarray(inp["L0_attn_in_proj_w"], np.float32).T)
    convw = np.asarray(inp["L0_attn_conv_w"], np.float32)
    convb = np.asarray(inp["L0_attn_conv_b"], np.float32).reshape(DI, 1)
    XPT = np.ascontiguousarray(np.asarray(inp["L0_attn_x_proj_w"], np.float32).T)
    wdt = np.ascontiguousarray(np.asarray(inp["L0_attn_dt_proj_w"], np.float32))
    bdt = np.asarray(inp["L0_attn_dt_proj_b"], np.float32).reshape(DI, 1)
    alog = np.asarray(inp["L0_attn_A_log"], np.float32)
    Dv = np.asarray(inp["L0_attn_D"], np.float32).reshape(DI, 1)

    xs, wts, xpts, diags, cbs, wdts, bdts, alogs, ds = [], [], [], [], [], [], [], [], []
    ar = np.arange(128)
    for core in range(8):
        b, h = core // 2, core % 2
        own = h * DHALF
        oth = (1 - h) * DHALF
        chan = np.concatenate([np.arange(own, own + DHALF), np.arange(oth, oth + DHALF)])
        xs.append(x[b])
        wts.append(np.concatenate(
            [WTi[:, chan], WTi[:, DI + own: DI + own + DHALF]], axis=1))
        xpts.append(XPT[chan, :])
        cw = convw[chan, 0, :]
        dg = np.zeros((16, 4, 128, 128), np.float32)
        for cc in range(16):
            for k in range(4):
                dg[cc, k, ar, ar] = cw[cc * 128 + ar, k]
        diags.append(dg)
        cbs.append(convb[chan])
        wdts.append(wdt[own:own + DHALF])
        bdts.append(bdt[own:own + DHALF])
        alogs.append(alog[own:own + DHALF])
        ds.append(Dv[own:own + DHALF])

    d_x = p.din("x", xs)
    d_wt = p.din("wt", wts)
    d_xpt = p.din("xpt", xpts)
    d_diag = p.din("diag", diags)
    d_cb = p.din("cb", cbs)
    d_wdt = p.din("wdt", wdts)
    d_bdt = p.din("bdt", bdts)
    d_alog = p.din("alog", alogs)
    d_dv = p.din("dv", ds)
    d_idb = p.din("identb", [np.eye(128, dtype=BF)] * 8)
    t_dgb = p.dtmp("dgb", (16, 4, 128, 128), BF16)

    sb = lambda shape, dt=F32: p.sb(shape, dt, es=esA)
    ps = lambda shape, dt=F32: p.ps(shape, dt, es=esA)

    scratch2 = sb((128, 8 * TCA))
    stage = scratch2[:, 0:1536]
    sq = scratch2[:, 0:1024]
    w_in = [sb((128, 3072), BF16) for _ in range(8)]
    w_xp = sb((128, 16 * 33), BF16)
    w_dgb = [sb((128, 4 * 128), BF16) for _ in range(2)]
    dgstage = sb((128, 128), BF16)
    cb_s = sb((128, 16))
    wdt_s = sb((128, 8)); bdt_s = sb((128, 8))
    a_s = sb((128, 8 * 16)); dv_s = sb((128, 8))
    ones_s = sb((65, 128))
    epscol = sb((128, 1))
    identb = sb((128, 128), BF16)
    xt = scratch2[:, 1024:2048]
    mcol = sb((128, 4)); rcol = sb((128, 4)); tmpc = sb((128, 4))
    lnxt = sb((128, 1024), BF16)
    lnxf = sb((128, 8 * TCA), BF16)
    xis = sb((128, 16 * (TCA + 3)), BF16)
    zs = sb((128, 8 * TCA), BF16)
    xco = sb((128, 8 * TCA))
    xcbf = [sb((128, TCA), BF16) for _ in range(2)]
    xp_s = sb((33, TCA))
    xp_rows = sb((65, 11 * TCA))

    def xprow(r):
        return xp_rows[32 * (r % 3): 32 * (r % 3) + 1,
                       (r // 3) * TCA:(r // 3 + 1) * TCA]

    def onesrow(r):
        return ones_s[32 * (r % 3): 32 * (r % 3) + 1, :]
    dts = sb((128, 8 * TCA))
    dtxdc = sb((128, TCA))
    bigs = sb((128, NG * TCA))
    hstate = sb((128, 8 * 16))
    yscr = sb((128, TCA)); y2scr = sb((128, TCA))
    zgate = sb((128, TCA))
    ygout = sb((128, TCA), BF16)

    pA = ps((128, 512)); pB = ps((128, 512))
    pXP = ps((33, 512))
    pT = ps((128, 128), BF16)
    pR = [ps((128, 512)) for _ in range(2)]
    pC = [ps((128, 512)) for _ in range(2)]

    st = p.step
    dma = p.dma

    dma(identb[:], d_idb)
    stg2 = [scratch2[:, 0:1536], scratch2[:, 1536:3072]]
    for jc in range(8):
        for half in range(2):
            sgi = stg2[(jc * 2 + half) % 2]
            dma(sgi, d_wt[jc * 128:(jc + 1) * 128, half * 1536:(half + 1) * 1536])
            st("vector", (lambda jc=jc, half=half, sgi=sgi: (lambda e: e.tensor_copy(
                w_in[jc][:, half * 1536:(half + 1) * 1536], sgi)))())
    for dc in range(16):
        sgi = scratch2[:, (dc % 2) * 64:(dc % 2) * 64 + 33]
        dma(sgi, d_xpt[dc * 128:(dc + 1) * 128, :])
        st("vector", (lambda dc=dc, sgi=sgi: (lambda e: e.tensor_copy(
            w_xp[:, dc * 33:(dc + 1) * 33], sgi)))())
    for cc in range(16):
        for k in range(4):
            j = (cc * 4 + k) % 2
            sgi = scratch2[:, 256 + j * 128: 256 + (j + 1) * 128]
            dma(sgi, d_diag[cc, k])
            st("vector", (lambda sgi=sgi: (lambda e: e.tensor_copy(
                dgstage[:], sgi)))())
            dma(t_dgb[cc, k], dgstage[:])
    for cc in range(16):
        dma(cb_s[:, cc:cc + 1], d_cb[cc * 128:(cc + 1) * 128, :])
    for dc in range(8):
        dma(wdt_s[:, dc:dc + 1], d_wdt[dc * 128:(dc + 1) * 128, :])
        dma(bdt_s[:, dc:dc + 1], d_bdt[dc * 128:(dc + 1) * 128, :])
        dma(dv_s[:, dc:dc + 1], d_dv[dc * 128:(dc + 1) * 128, :])
        dma(a_s[:, dc * 16:(dc + 1) * 16], d_alog[dc * 128:(dc + 1) * 128, :])
    st("scalar", lambda e: e.activation(a_s[:], a_s[:], AF.Exp))
    st("vector", lambda e: e.tensor_scalar_mul(a_s[:], a_s[:], -1.0))
    st("vector", lambda e: e.memset(ones_s[:], 1.0))
    st("vector", lambda e: e.memset(epscol[:], EPS))
    st("vector", lambda e: e.memset(hstate[:], 0.0))
    st("vector", lambda e: e.memset(xis[:], 0.0))

    for c in range(NCA):
        t0 = c * TCA
        for i in range(NTILE):
            dma(xt, d_x[t0 + i * 128: t0 + (i + 1) * 128, :])
            _ln_stats(p, xt, mcol, rcol, sq, tmpc, i % 4, epscol)
            st("vector", (lambda i=i: (lambda e: e.tensor_scalar(
                lnxt[:], xt, mcol[:, i % 4:i % 4 + 1],
                rcol[:, i % 4:i % 4 + 1], OP.subtract, OP.mult)))())
            for jc in range(8):
                st("tensor", (lambda jc=jc: (lambda e: e.transpose(
                    pT[:], lnxt[:, jc * 128:(jc + 1) * 128], identb[:])))())
                st("scalar", (lambda i=i, jc=jc: (lambda e: e.activation(
                    lnxf[:, jc * TCA + i * 128: jc * TCA + (i + 1) * 128], pT[:],
                    AF.Copy)))())

        for mc in range(24):
            for jc in range(8):
                st("tensor", (lambda mc=mc, jc=jc: (lambda e: e.matmul(
                    pA[:, 0:TCA], w_in[jc][:, mc * 128:(mc + 1) * 128],
                    lnxf[:, jc * TCA:(jc + 1) * TCA],
                    start=(jc == 0), stop=(jc == 7))))())
            if mc < 16:
                o = mc * (TCA + 3) + 3
                st("scalar", (lambda o=o: (lambda e: e.activation(
                    xis[:, o: o + TCA], pA[:, 0:TCA], AF.Copy)))())
            else:
                zc = mc - 16
                st("scalar", (lambda zc=zc: (lambda e: e.activation(
                    zs[:, zc * TCA:(zc + 1) * TCA], pA[:, 0:TCA], AF.Silu)))())

        for cc in range(16):
            base = cc * (TCA + 3)
            dma(w_dgb[cc % 2][:].rearrange("a (k b) -> a k b", b=128),
                t_dgb[cc].rearrange("k a b -> a k b"))
            for k in range(4):
                st("tensor", (lambda cc=cc, k=k, base=base: (lambda e: e.matmul(
                    pB[:, 0:TCA], w_dgb[cc % 2][:, k * 128:(k + 1) * 128],
                    xis[:, base + k: base + k + TCA],
                    start=(k == 0), stop=(k == 3))))())
            st("vector", (lambda base=base: (lambda e: e.tensor_copy(
                xis[:, base: base + 3], xis[:, base + TCA: base + TCA + 3])))())
            st("scalar", (lambda cc=cc: (lambda e: e.activation(
                xcbf[cc % 2][:], pB[:, 0:TCA], AF.Silu, bias=cb_s[:, cc:cc + 1])))())
            if cc < 8:
                st("scalar", (lambda cc=cc: (lambda e: e.activation(
                    xco[:, cc * TCA:(cc + 1) * TCA], pB[:, 0:TCA], AF.Silu,
                    bias=cb_s[:, cc:cc + 1])))())
            st("tensor", (lambda cc=cc: (lambda e: e.matmul(
                pXP[:, 0:TCA], w_xp[:, cc * 33:(cc + 1) * 33], xcbf[cc % 2][:],
                start=(cc == 0), stop=(cc == 15))))())
        st("scalar", lambda e: e.activation(xp_s[:], pXP[:, 0:TCA], AF.Copy))
        for r in range(33):
            dma(xprow(r), xp_s[r:r + 1, :])

        st("tensor", lambda e: e.matmul(pR[0][:, 0:TCA], onesrow(0),
                                        xprow(0), start=True, stop=True))
        for dc in range(8):
            st("scalar", (lambda dc=dc: (lambda e: e.activation(
                dts[:, dc * TCA:(dc + 1) * TCA], pR[0][:, 0:TCA], AF.Identity,
                scale=wdt_s[:, dc:dc + 1], bias=bdt_s[:, dc:dc + 1])))())
        st("scalar", lambda e: e.activation(scratch2[:], dts[:], AF.Abs))
        st("scalar", lambda e: e.activation(scratch2[:], scratch2[:], AF.Exp, scale=-1.0))
        st("scalar", lambda e: e.activation(scratch2[:], scratch2[:], AF.Ln, bias=1.0))
        st("scalar", lambda e: e.activation(dts[:], dts[:], AF.Relu))
        st("vector", lambda e: e.tensor_add(dts[:], dts[:], scratch2[:]))

        for dc in range(8):
            dsl = dts[:, dc * TCA:(dc + 1) * TCA]
            st("vector", (lambda dc=dc: (lambda e: e.tensor_tensor(
                dtxdc[:], dts[:, dc * TCA:(dc + 1) * TCA],
                xco[:, dc * TCA:(dc + 1) * TCA], OP.mult)))())
            st("vector", lambda e: e.memset(yscr[:], 0.0))
            for g in range(2):
                for nl in range(NG):
                    n = g * NG + nl
                    dascr = scratch2[:, (nl % 2) * TCA:(nl % 2 + 1) * TCA]
                    dbscr = scratch2[:, (2 + nl % 2) * TCA:(3 + nl % 2) * TCA]
                    st("tensor", (lambda n=n, nl=nl: (lambda e: e.matmul(
                        pR[nl % 2][:, 0:TCA], onesrow(1 + n),
                        xprow(1 + n), start=True, stop=True)))())
                    st("scalar", (lambda dsl=dsl, dc=dc, n=n, dascr=dascr:
                        (lambda e: e.activation(
                            dascr, dsl, AF.Exp,
                            scale=a_s[:, dc * 16 + n: dc * 16 + n + 1])))())
                    st("vector", (lambda nl=nl, dbscr=dbscr: (lambda e: e.tensor_tensor(
                        dbscr, dtxdc[:], pR[nl % 2][:, 0:TCA], OP.mult)))())
                    st("vector", (lambda dc=dc, n=n, nl=nl, dascr=dascr, dbscr=dbscr:
                        (lambda e: e.tensor_tensor_scan(
                            bigs[:, nl: NG * TCA: NG], dascr, dbscr,
                            hstate[:, dc * 16 + n: dc * 16 + n + 1], OP.mult, OP.add)))())
                st("vector", (lambda dc=dc, g=g: (lambda e: e.tensor_copy(
                    hstate[:, dc * 16 + g * NG: dc * 16 + (g + 1) * NG],
                    bigs[:, (TCA - 1) * NG: TCA * NG])))())
                for nl in range(NG):
                    n = g * NG + nl
                    st("tensor", (lambda n=n, nl=nl: (lambda e: e.matmul(
                        pC[nl % 2][:, 0:TCA], onesrow(17 + n),
                        xprow(17 + n), start=True, stop=True)))())
                    st("vector", (lambda nl=nl: (lambda e: e.tensor_tensor(
                        bigs[:, nl: NG * TCA: NG], bigs[:, nl: NG * TCA: NG],
                        pC[nl % 2][:, 0:TCA], OP.mult)))())
                st("vector", lambda e: e.tensor_reduce(
                    y2scr[:].rearrange("p (t o) -> p t o", o=1),
                    bigs[:].rearrange("p (t n) -> p t n", n=NG), AX.X, OP.add))
                st("gpsimd", lambda e: e.tensor_add(yscr[:], yscr[:], y2scr[:]))
            st("vector", (lambda dc=dc: (lambda e: e.scalar_tensor_tensor(
                yscr[:], xco[:, dc * TCA:(dc + 1) * TCA], dv_s[:, dc:dc + 1],
                yscr[:], OP.mult, OP.add)))())
            st("gpsimd", (lambda dc=dc: (lambda e: e.tensor_tensor(
                ygout[:], yscr[:], zs[:, dc * TCA:(dc + 1) * TCA], OP.mult)))())
            dma(t_ygl[dc * 128:(dc + 1) * 128, t0:t0 + TCA], ygout[:], eng="gpsimd")
    return scratch2


def make_publish(p, t_ygl, t_ygsh, t_fl, d_nc, scratch2, flag1):
    nc = p.nc
    stg = scratch2[:].bitcast(BF16)
    lA = p.es.enter_context(nc.semaphore("pub_lA"))
    lB = p.es.enter_context(nc.semaphore("pub_lB"))

    INC = 2 * 16

    def emitter(eng, sem, base):
        eng.dma_start(out=flag1[:], in_=d_nc).then_inc(sem, 16)
        pid = eng.partition_id()
        with eng.register("rpar1") as rpar:
            eng.reg_alu(rpar, pid, 2, OP.mod)

            def branch(me):
                lanes = [lA, lB]
                cnt = [0, 0]
                for rb in range(8):
                    ln = rb % 2
                    if cnt[ln]:
                        eng.wait_ge(lanes[ln], cnt[ln])
                    eng.dma_start(
                        out=stg[:, ln * 4096:(ln + 1) * 4096],
                        in_=t_ygl[rb * 128:(rb + 1) * 128, :]
                    ).then_inc(lanes[ln], 16)
                    cnt[ln] += 16
                    eng.wait_ge(lanes[ln], cnt[ln])
                    eng.dma_start(
                        out=t_ygsh[me, rb * 128:(rb + 1) * 128, :],
                        in_=stg[:, ln * 4096:(ln + 1) * 4096]
                    ).then_inc(lanes[ln], 16)
                    cnt[ln] += 16
                eng.wait_ge(lanes[0], cnt[0])
                eng.wait_ge(lanes[1], cnt[1])
                eng.wait_ge(sem, base + 16)
                eng.dma_start(out=t_fl[0:1, me * 2:me * 2 + 2],
                              in_=flag1[:]).then_inc(sem, 16)

            with eng.If_eq(rpar, 0):
                branch(0)
            with eng.Else():
                branch(1)
        eng.wait_ge(sem, base + INC)

    return emitter, INC


def make_windows(p, t_ygsh, t_fl, t_ygw, d_nc, zband, bstages):
    nc = p.nc
    flag1b = p.sb((1, 2), I32)
    flagp = p.sb((1, 2), I32)
    wbufs = [bstages[0][:].bitcast(BF16), bstages[1][:].bitcast(BF16)]
    s_poll = p.es.enter_context(nc.semaphore("s_poll"))
    lA = p.es.enter_context(nc.semaphore("win_lA"))
    lB = p.es.enter_context(nc.semaphore("win_lB"))

    INC = 2 * 16
    WV = 2112

    def emitter(eng, sem, base):
        eng.dma_start(out=flag1b[:], in_=d_nc).then_inc(sem, 16)
        eng.wait_ge(sem, base + 16)
        pid = eng.partition_id()
        with eng.register("rpar2") as rpar:
            eng.reg_alu(rpar, pid, 2, OP.mod)

            def branch(me, pe):
                tag = f"w{me}"
                with eng.register(f"rc{tag}") as rc, \
                     eng.register(f"rv{tag}") as rv, \
                     eng.register(f"rw0{tag}") as rw0, \
                     eng.register(f"rw1{tag}") as rw1, \
                     eng.register(f"rn0{tag}") as rn0, \
                     eng.register(f"rn1{tag}") as rn1, \
                     eng.register(f"rc2{tag}") as rc2:
                    eng.load(rn0, flag1b[0:1, 0:1])
                    eng.load(rn1, flag1b[0:1, 1:2])
                    eng.reg_mov(rc, 1)
                    eng.reg_mov(rv, 16)
                    with eng.While(rc):
                        eng.dma_start(out=flagp[:],
                                      in_=t_fl[0:1, pe * 2:pe * 2 + 2]
                                      ).then_inc(s_poll, 16)
                        eng.wait_ge(s_poll, rv)
                        eng.reg_add(rv, rv, 16)
                        eng.load(rw0, flagp[0:1, 0:1])
                        eng.load(rw1, flagp[0:1, 1:2])
                        eng.reg_alu(rc, rw0, rn0, OP.not_equal)
                        eng.reg_alu(rc2, rw1, rn1, OP.not_equal)
                        eng.reg_alu(rc, rc, rc2, OP.logical_or)
                lanes = [lA, lB]
                cnt = [0, 0]
                zlo, zhi = (0, 64) if me == 0 else (2112, 2176)
                for rb in range(16):
                    ln = rb % 2
                    eng.dma_start(out=t_ygw[rb * 128:(rb + 1) * 128, zlo:zhi],
                                  in_=zband[:]).then_inc(lanes[ln], 16)
                    cnt[ln] += 16
                scol = 0 if me == 0 else S - WV
                dcol = 64 if me == 0 else 0
                for s in range(2):
                    for rb in range(8):
                        ln = (s * 8 + rb) % 2
                        eng.wait_ge(lanes[ln], cnt[ln])
                        eng.dma_start(
                            out=wbufs[ln][:, 0:WV],
                            in_=t_ygsh[s, rb * 128:(rb + 1) * 128, scol:scol + WV]
                        ).then_inc(lanes[ln], 16)
                        cnt[ln] += 16
                        eng.wait_ge(lanes[ln], cnt[ln])
                        eng.dma_start(
                            out=t_ygw[s * DHALF + rb * 128: s * DHALF + (rb + 1) * 128,
                                      dcol:dcol + WV],
                            in_=wbufs[ln][:, 0:WV]
                        ).then_inc(lanes[ln], 16)
                        cnt[ln] += 16
                eng.wait_ge(lanes[0], cnt[0])
                eng.wait_ge(lanes[1], cnt[1])
                eng.dma_start(out=t_fl[0:1, 8 + me:9 + me],
                              in_=flag1b[0:1, 0:1]).then_inc(sem, 16)

            with eng.If_eq(rpar, 0):
                branch(0, 1)
            with eng.Else():
                branch(1, 0)
        eng.wait_ge(sem, base + INC)

    return emitter, INC


def emit_ffn_weights(p, inp, esB):
    x = np.asarray(inp["L0_attn_x"], np.float32)
    WOT = np.ascontiguousarray(np.asarray(inp["L0_attn_out_proj_w"], np.float32).T)
    W1T = np.ascontiguousarray(np.asarray(inp["L0_ffn_w1"], np.float32).T)
    W2T = np.ascontiguousarray(np.asarray(inp["L0_ffn_w2"], np.float32).T)
    fcw = np.asarray(inp["L0_ffn_conv_w"], np.float32)
    fcb = np.asarray(inp["L0_ffn_conv_b"], np.float32).reshape(DH, 1)

    fdiag = np.zeros((32, 3, 128, 128), np.float32)
    ar = np.arange(128)
    for m in range(32):
        for k in range(3):
            fdiag[m, k, ar, ar] = fcw[m * 128 + ar, 0, k]

    xps = []
    for core in range(8):
        b, th = core // 2, core % 2
        t0 = th * THALF - PAD
        xp = np.zeros((TB, D), np.float32)
        lo, hi = max(0, t0), min(S, t0 + TB)
        xp[lo - t0:hi - t0, :] = x[b][lo:hi, :]
        xps.append(xp)

    d_xw = p.din("xw", xps)
    d_wot = p.din("wot", [WOT] * 8)
    d_w1t = p.din("w1t", [W1T] * 8)
    d_w2t = p.din("w2t", [W2T] * 8)
    d_fdg = p.din("fdg", [fdiag] * 8)
    d_fcb = p.din("fcb", [fcb] * 8)
    d_idb2 = p.din("identb2", [np.eye(128, dtype=BF)] * 8)
    d_idf = p.din("identf", [np.eye(128, dtype=np.float32)] * 8)

    o_out = p.dout("out", (THALF, D))

    t_x2 = p.dtmp("x2s", (TB, D))
    t_xq1 = p.dtmp("xq1s", (8, 128, TB), BF16)
    t_s1i = p.dtmp("s1is", (TB, 1))
    t_s2 = p.dtmp("s2s", (THALF, 1)); t_s2i = p.dtmp("s2is", (THALF, 1))
    t_wq1 = p.dtmp("wq1s", (32, 8, 128, 128), BF16)
    t_wq2 = p.dtmp("wq2s", (8, 32, 128, 128), BF16)
    t_wot = p.dtmp("wots", (8, 16, 128, 128), BF16)
    t_fdg = p.dtmp("fdgs", (32, 3, 128, 128), BF16)

    sb = lambda shape, dt=F32: p.sb(shape, dt, es=esB)
    ps = lambda shape, dt=F32: p.ps(shape, dt, es=esB)

    stage = sb((128, 2048))
    stage2 = sb((128, 2048))
    qi32 = sb((128, 2048), I32)
    qbf = sb((128, TB), BF16)
    acc8 = sb((128, 34))
    onescol = sb((128, 1))
    ones_s = sb((1, 128))
    epscol = sb((128, 1))
    mw1col = sb((128, 1))
    mw2col = sb((128, 1))
    minv = sb((128, 1))
    identb = sb((128, 128), BF16)
    identf = sb((128, 128))
    fcb_s = sb((128, 32))
    ygbf = sb((128, 16 * 512), BF16)
    wotb = [sb((128, 16 * 128), BF16) for _ in range(2)]
    afm = sb((128, 512))
    xt1 = sb((128, 1024))
    x2buf = sb((128, 4 * 1024))
    mcol = sb((128, 2)); rcol = sb((128, 2)); tmpc = sb((128, 2))
    sq = sb((128, 1024))
    u_scr = sb((128, 1024))
    amax = sb((128, 1)); s1col = sb((128, 1)); rscol = sb((128, 1))
    s1icol = sb((128, 1))
    xq1i = sb((128, 1024), I32)
    xq1t = sb((128, 1024), BF16)
    s1irow = sb((1, TW)); arow = sb((1, TW))
    arep = sb((128, TW))
    xq1b = sb((128, 8 * TW), BF16)
    w1b = [sb((128, 8 * 128), BF16) for _ in range(2)]
    h1s = sb((128, 32 * TW), BF16)
    fdgb = [sb((128, 3 * 128), BF16) for _ in range(2)]
    h2 = sb((128, 32 * TCB))
    macc = sb((128, TCB))
    habs = sb((128, TCB))
    mts = sb((128, 128))
    mxc = sb((128, 2)); s2col = sb((128, 2)); s2icol = sb((128, 2))
    s2row = sb((1, TCB)); brow = sb((1, TCB))
    g2rep = sb((128, TCB)); brep = sb((128, TCB))
    xq2i = sb((128, TCB), I32)
    xq2 = sb((128, 32 * TCB), BF16)
    w2b = [sb((128, 32 * 128), BF16) for _ in range(2)]
    ffm = sb((128, TCB))
    x2c = sb((128, 2 * 1024))
    ot = sb((128, 2 * 1024))

    pA = ps((128, 512)); pB = ps((128, 512))
    pCv = ps((128, 512))
    pT32 = ps((128, 128)); pTb = ps((128, 128), BF16)
    pR = ps((128, 512)); pS = ps((128, 8))

    st = p.step
    dma = p.dma

    dma(identb[:], d_idb2)
    dma(identf[:], d_idf)
    st("vector", lambda e: e.memset(ones_s[:], 1.0))
    st("vector", lambda e: e.memset(onescol[:], 1.0))
    st("vector", lambda e: e.memset(epscol[:], EPS))
    for m in range(32):
        dma(fcb_s[:, m:m + 1], d_fcb[m * 128:(m + 1) * 128, :])

    def wquant(d_wt, ntiles, width, mcol_, t_scr, nval):
        nh = (width + 2047) // 2048
        hw = width // nh
        for i in range(ntiles):
            for q in range(nh):
                sgi = (stage if (i * nh + q) % 2 == 0 else stage2)[:, 0:hw]
                dma(sgi, d_wt[i * 128:(i + 1) * 128, q * hw:(q + 1) * hw])
                st("vector", (lambda i=i, q=q, hw=hw, nh=nh, sgi=sgi:
                    (lambda e: e.tensor_reduce(
                        acc8[:, i * nh + q: i * nh + q + 1], sgi, AX.X,
                        OP.add, apply_absolute_value=True)))())
        ncols = ntiles * nh
        st("vector", (lambda ncols=ncols: (lambda e: e.tensor_reduce(
            acc8[:, 33:34].rearrange("p (o q) -> p o q", q=1),
            acc8[:, 0:ncols].rearrange("p (o t) -> p o t", o=1), AX.X, OP.add)))())
        st("tensor", lambda e: e.matmul(pS[0:1, 0:1], onescol[:], acc8[:, 33:34],
                                        start=True, stop=True))
        st("scalar", lambda e: e.activation(acc8[0:1, 32:33], pS[0:1, 0:1], AF.Copy))
        st("vector", (lambda nval=nval: (lambda e: e.tensor_scalar_mul(
            acc8[0:1, 32:33], acc8[0:1, 32:33], 1.0 / nval)))())
        st("vector", lambda e: e.tensor_scalar_max(acc8[0:1, 32:33], acc8[0:1, 32:33], 1e-5))
        st("tensor", lambda e: e.matmul(pS[:, 1:2], ones_s[0:1, :], acc8[0:1, 32:33],
                                        start=True, stop=True))
        st("scalar", (lambda mcol_=mcol_: (lambda e: e.activation(
            mcol_[:], pS[:, 1:2], AF.Copy)))())
        st("vector", (lambda mcol_=mcol_: (lambda e: e.reciprocal(minv[:], mcol_[:])))())
        for i in range(ntiles):
            for q in range(nh):
                sgi = (stage if (i * nh + q) % 2 == 0 else stage2)[:, 0:hw]
                dma(sgi, d_wt[i * 128:(i + 1) * 128, q * hw:(q + 1) * hw])
                st("vector", (lambda hw=hw, sgi=sgi: (lambda e: e.tensor_scalar(
                    sgi, sgi, minv[:, 0:1], 1.49, OP.mult, OP.min)))())
                st("vector", (lambda hw=hw, sgi=sgi: (lambda e: e.tensor_scalar(
                    qi32[:, 0:hw], sgi, -1.49, None, OP.max)))())
                st("vector", (lambda hw=hw: (lambda e: e.tensor_copy(
                    qbf[:, 0:hw], qi32[:, 0:hw])))())
                mph = hw // 128
                dma(t_scr[q * mph:(q + 1) * mph, i].rearrange("m j h -> j m h"),
                    qbf[:, 0:hw].rearrange("j (m h) -> j m h", h=128))
    wquant(d_w1t, 8, 4096, mw1col, t_wq1, DH * D)
    wquant(d_w2t, 32, 1024, mw2col, t_wq2, DH * D)
    for i in range(16):
        dma(stage[:, 0:1024], d_wot[i * 128:(i + 1) * 128, :])
        st("vector", lambda e: e.tensor_copy(qbf[:, 0:1024], stage[:, 0:1024]))
        dma(t_wot[:, i].rearrange("m d c -> d m c"),
            qbf[:, 0:1024].rearrange("d (m c) -> d m c", c=128))
    for m in range(32):
        for k in range(3):
            dma(stage[:, 0:128], d_fdg[m, k])
            st("vector", lambda e: e.tensor_copy(qbf[:, 0:128], stage[:, 0:128]))
            dma(t_fdg[m, k], qbf[:, 0:128])

    return dict(locals())


def emit_ffn_main(p, ctx, t_ygw):
    st = ctx['st']
    dma = ctx['dma']
    d_xw = ctx['d_xw']
    o_out = ctx['o_out']
    t_x2 = ctx['t_x2']
    t_xq1 = ctx['t_xq1']
    t_s1i = ctx['t_s1i']
    t_s2 = ctx['t_s2']
    t_s2i = ctx['t_s2i']
    t_wq1 = ctx['t_wq1']
    t_wq2 = ctx['t_wq2']
    t_wot = ctx['t_wot']
    t_fdg = ctx['t_fdg']
    stage = ctx['stage']
    qi32 = ctx['qi32']
    qbf = ctx['qbf']
    acc8 = ctx['acc8']
    onescol = ctx['onescol']
    ones_s = ctx['ones_s']
    epscol = ctx['epscol']
    mw1col = ctx['mw1col']
    mw2col = ctx['mw2col']
    minv = ctx['minv']
    identb = ctx['identb']
    identf = ctx['identf']
    fcb_s = ctx['fcb_s']
    ygbf = ctx['ygbf']
    wotb = ctx['wotb']
    afm = ctx['afm']
    xt1 = ctx['xt1']
    x2buf = ctx['x2buf']
    mcol = ctx['mcol']
    rcol = ctx['rcol']
    tmpc = ctx['tmpc']
    sq = ctx['sq']
    u_scr = ctx['u_scr']
    amax = ctx['amax']
    s1col = ctx['s1col']
    rscol = ctx['rscol']
    s1icol = ctx['s1icol']
    xq1i = ctx['xq1i']
    xq1t = ctx['xq1t']
    s1irow = ctx['s1irow']
    arow = ctx['arow']
    arep = ctx['arep']
    xq1b = ctx['xq1b']
    w1b = ctx['w1b']
    h1s = ctx['h1s']
    fdgb = ctx['fdgb']
    h2 = ctx['h2']
    macc = ctx['macc']
    habs = ctx['habs']
    mts = ctx['mts']
    mxc = ctx['mxc']
    s2col = ctx['s2col']
    s2icol = ctx['s2icol']
    s2row = ctx['s2row']
    brow = ctx['brow']
    g2rep = ctx['g2rep']
    brep = ctx['brep']
    xq2i = ctx['xq2i']
    xq2 = ctx['xq2']
    w2b = ctx['w2b']
    ffm = ctx['ffm']
    x2c = ctx['x2c']
    ot = ctx['ot']
    pA = ctx['pA']
    pB = ctx['pB']
    pT32 = ctx['pT32']
    pTb = ctx['pTb']
    pR = ctx['pR']
    pS = ctx['pS']
    pCv = ctx['pCv']
    stage2 = ctx['stage2']
    fcs = [(0, 512), (512, 512), (1024, 512), (1536, 512), (2048, 128)]
    for f0, fc in fcs:
        for dc in range(16):
            dma(ygbf[:, dc * 512: dc * 512 + fc],
                t_ygw[dc * 128:(dc + 1) * 128, f0:f0 + fc])
        nt = fc // 128
        for dm in range(8):
            wob = wotb[dm % 2]
            pAc = pA if dm % 2 == 0 else pB
            dma(wob[:].rearrange("d (dc c) -> d dc c", c=128),
                t_wot[dm].rearrange("dc d c -> d dc c"))
            for dc in range(16):
                st("tensor", (lambda dm=dm, dc=dc, fc=fc, wob=wob, pAc=pAc:
                    (lambda e: e.matmul(
                        pAc[:, 0:fc], wob[:, dc * 128:(dc + 1) * 128],
                        ygbf[:, dc * 512: dc * 512 + fc],
                        start=(dc == 0), stop=(dc == 15))))())
            st("scalar", (lambda fc=fc, pAc=pAc: (lambda e: e.activation(
                afm[:, 0:fc], pAc[:, 0:fc], AF.Copy)))())
            for ts in range(nt):
                st("tensor", (lambda ts=ts: (lambda e: e.transpose(
                    pT32[:], afm[:, ts * 128:(ts + 1) * 128], identf[:])))())
                st("vector", (lambda dm=dm, ts=ts: (lambda e: e.tensor_copy(
                    x2buf[:, ts * 1024 + dm * 128: ts * 1024 + (dm + 1) * 128],
                    pT32[:])))())
        for ts in range(nt):
            ti = f0 // 128 + ts
            xb = x2buf[:, ts * 1024:(ts + 1) * 1024]
            dma(xt1[:], d_xw[ti * 128:(ti + 1) * 128, :])
            st("vector", (lambda xb=xb: (lambda e: e.tensor_tensor(
                xb, xb, xt1[:], OP.add)))())
            dma(t_x2[ti * 128:(ti + 1) * 128, :], xb, eng="gpsimd")
            _ln_stats(p, xb, mcol, rcol, sq, tmpc, 0, epscol)
            st("vector", (lambda xb=xb: (lambda e: e.tensor_scalar(
                u_scr[:], xb, mcol[:, 0:1], None, OP.subtract)))())
            st("vector", lambda e: e.tensor_reduce(
                amax[:], u_scr[:], AX.X, OP.max, apply_absolute_value=True))
            st("vector", lambda e: e.tensor_tensor(amax[:], amax[:], rcol[:, 0:1], OP.mult))
            st("vector", lambda e: e.tensor_scalar_max(amax[:], amax[:], 1e-5))
            st("vector", lambda e: e.reciprocal(s1col[:], amax[:]))
            st("vector", lambda e: e.tensor_scalar_mul(s1col[:], s1col[:], 127.0))
            st("vector", lambda e: e.tensor_tensor(rscol[:], rcol[:, 0:1], s1col[:], OP.mult))
            st("vector", lambda e: e.tensor_scalar_mul(s1icol[:], amax[:], 1.0 / 127.0))
            dma(t_s1i[ti * 128:(ti + 1) * 128, :], s1icol[:], eng="gpsimd")
            st("vector", (lambda xb=xb: (lambda e: e.tensor_scalar(
                xq1i[:], xb, mcol[:, 0:1], rscol[:, 0:1], OP.subtract, OP.mult)))())
            st("vector", lambda e: e.tensor_copy(xq1t[:], xq1i[:]))
            for jc in range(8):
                st("tensor", (lambda jc=jc: (lambda e: e.transpose(
                    pTb[:], xq1t[:, jc * 128:(jc + 1) * 128], identb[:])))())
                st("scalar", (lambda jc=jc: (lambda e: e.activation(
                    qbf[:, 0:128], pTb[:], AF.Copy)))())
                dma(t_xq1[jc, :, ti * 128:(ti + 1) * 128], qbf[:, 0:128], eng="gpsimd")

    for c in range(NCB):
        q0 = PAD + c * TCB - 1
        dma(s1irow[:], t_s1i.rearrange("t o -> o t")[:, q0:q0 + TW])
        st("vector", lambda e: e.tensor_scalar(
            arow[:], s1irow[:], mw1col[0:1, 0:1], None, OP.mult))
        st("tensor", lambda e: e.matmul(pR[:, 0:TW], ones_s[0:1, :], arow[:],
                                        start=True, stop=True))
        st("scalar", lambda e: e.activation(arep[:], pR[:, 0:TW], AF.Copy))
        for jc in range(8):
            dma(xq1b[:, jc * TW:(jc + 1) * TW], t_xq1[jc, :, q0:q0 + TW])
        for m in range(32):
            pH = pA if m % 2 == 0 else pB
            dma(w1b[m % 2][:].rearrange("j (jc h) -> j jc h", h=128),
                t_wq1[m].rearrange("jc j h -> j jc h"))
            for jc in range(8):
                st("tensor", (lambda m=m, jc=jc, pH=pH: (lambda e: e.matmul(
                    pH[:, 0:TW], w1b[m % 2][:, jc * 128:(jc + 1) * 128],
                    xq1b[:, jc * TW:(jc + 1) * TW],
                    start=(jc == 0), stop=(jc == 7))))())
            st("vector", (lambda m=m, pH=pH: (lambda e: e.tensor_tensor(
                h1s[:, m * TW:(m + 1) * TW], pH[:, 0:TW], arep[:], OP.mult)))())
        st("vector", lambda e: e.memset(macc[:], 0.0))
        for m in range(32):
            dma(fdgb[m % 2][:].rearrange("a (k b) -> a k b", b=128),
                t_fdg[m].rearrange("k a b -> a k b"))
            for k in range(3):
                st("tensor", (lambda m=m, k=k: (lambda e: e.matmul(
                    pCv[:, 0:TCB], fdgb[m % 2][:, k * 128:(k + 1) * 128],
                    h1s[:, m * TW + k: m * TW + k + TCB],
                    start=(k == 0), stop=(k == 2))))())
            st("scalar", (lambda m=m: (lambda e: e.activation(
                h2[:, m * TCB:(m + 1) * TCB], pCv[:, 0:TCB], AF.Silu,
                bias=fcb_s[:, m:m + 1])))())
            st("scalar", (lambda m=m: (lambda e: e.activation(
                habs[:], h2[:, m * TCB:(m + 1) * TCB], AF.Abs)))())
            st("vector", lambda e: e.tensor_tensor(macc[:], macc[:], habs[:], OP.max))
        for ts in range(TCB // 128):
            st("tensor", (lambda ts=ts: (lambda e: e.transpose(
                pT32[:], macc[:, ts * 128:(ts + 1) * 128], identf[:])))())
            st("vector", lambda e: e.tensor_copy(mts[:], pT32[:]))
            st("vector", (lambda ts=ts: (lambda e: e.tensor_reduce(
                mxc[:, ts:ts + 1], mts[:], AX.X, OP.max)))())
            st("vector", (lambda ts=ts: (lambda e: e.tensor_scalar_max(
                mxc[:, ts:ts + 1], mxc[:, ts:ts + 1], 1e-5)))())
            st("vector", (lambda ts=ts: (lambda e: e.reciprocal(
                s2col[:, ts:ts + 1], mxc[:, ts:ts + 1])))())
            st("vector", (lambda ts=ts: (lambda e: e.tensor_scalar_mul(
                s2col[:, ts:ts + 1], s2col[:, ts:ts + 1], 127.0)))())
            st("vector", (lambda ts=ts: (lambda e: e.tensor_scalar_mul(
                s2icol[:, ts:ts + 1], mxc[:, ts:ts + 1], 1.0 / 127.0)))())
            dma(t_s2[c * TCB + ts * 128: c * TCB + (ts + 1) * 128, :],
                s2col[:, ts:ts + 1], eng="gpsimd")
            dma(t_s2i[c * TCB + ts * 128: c * TCB + (ts + 1) * 128, :],
                s2icol[:, ts:ts + 1], eng="gpsimd")
        dma(s2row[:], t_s2.rearrange("t o -> o t")[:, c * TCB:(c + 1) * TCB])
        dma(brow[:], t_s2i.rearrange("t o -> o t")[:, c * TCB:(c + 1) * TCB])
        st("vector", lambda e: e.tensor_scalar(
            brow[:], brow[:], mw2col[0:1, 0:1], None, OP.mult))
        st("tensor", lambda e: e.matmul(pR[:, 0:TCB], ones_s[0:1, :], s2row[:],
                                        start=True, stop=True))
        st("scalar", lambda e: e.activation(g2rep[:], pR[:, 0:TCB], AF.Copy))
        st("tensor", lambda e: e.matmul(pR[:, 0:TCB], ones_s[0:1, :], brow[:],
                                        start=True, stop=True))
        st("scalar", lambda e: e.activation(brep[:], pR[:, 0:TCB], AF.Copy))
        for m in range(32):
            st("vector", (lambda m=m: (lambda e: e.tensor_tensor(
                xq2i[:], h2[:, m * TCB:(m + 1) * TCB], g2rep[:], OP.mult)))())
            st("vector", (lambda m=m: (lambda e: e.tensor_copy(
                xq2[:, m * TCB:(m + 1) * TCB], xq2i[:])))())
        dma(x2c[:, 0:1024], t_x2[q0 + 1: q0 + 1 + 128, :])
        dma(x2c[:, 1024:2048], t_x2[q0 + 1 + 128: q0 + 1 + 256, :])
        for dm in range(8):
            pH2 = pA if dm % 2 == 0 else pB
            dma(w2b[dm % 2][:].rearrange("h (hc c) -> h hc c", c=128),
                t_wq2[dm].rearrange("hc h c -> h hc c"))
            for hc in range(32):
                st("tensor", (lambda dm=dm, hc=hc, pH2=pH2: (lambda e: e.matmul(
                    pH2[:, 0:TCB], w2b[dm % 2][:, hc * 128:(hc + 1) * 128],
                    xq2[:, hc * TCB:(hc + 1) * TCB],
                    start=(hc == 0), stop=(hc == 31))))())
            st("vector", (lambda pH2=pH2: (lambda e: e.tensor_tensor(
                ffm[:], pH2[:, 0:TCB], brep[:], OP.mult)))())
            for ts in range(TCB // 128):
                st("tensor", (lambda dm=dm, ts=ts: (lambda e: e.transpose(
                    pT32[:], ffm[:, ts * 128:(ts + 1) * 128], identf[:])))())
                st("vector", (lambda dm=dm, ts=ts: (lambda e: e.tensor_tensor(
                    ot[:, ts * 1024 + dm * 128: ts * 1024 + (dm + 1) * 128],
                    pT32[:], x2c[:, ts * 1024 + dm * 128: ts * 1024 + (dm + 1) * 128],
                    OP.add)))())
        dma(o_out[c * TCB: c * TCB + 128, :], ot[:, 0:1024], eng="gpsimd")
        dma(o_out[c * TCB + 128: c * TCB + 256, :], ot[:, 1024:2048], eng="gpsimd")


def kernel(**inputs):
    global _dbg
    _dbg = {}
    p = Prog("F")
    t_ygl = p.dtmp("ygl", (DHALF, S), BF16)
    t_ygsh = p.dtmp("ygsh", (2, DHALF, S), BF16, shared=True)
    t_fl = p.dtmp("flg", (1, 16), I32, shared=True)
    t_ygw = p.dtmp("ygw", (DI, TB), BF16)
    rngv = np.random.default_rng()
    n0, n1 = (int(rngv.integers(1, 2**31 - 1)), int(rngv.integers(1, 2**31 - 1)))
    d_nc = p.din("nonce", [np.array([[n0, n1]], np.int32)] * 8)

    flag1 = p.sb((1, 2), I32)
    esA = contextlib.ExitStack()
    scratch2 = emit_mamba(p, inputs, t_ygl, esA)
    pub, INC1 = make_publish(p, t_ygl, t_ygsh, t_fl, d_nc, scratch2, flag1)
    p.raw("sync", pub, INC1)
    esA.close()
    zband = p.sb((128, 64), BF16)
    p.step("vector", lambda e: e.memset(zband[:], 0.0))
    ctx = emit_ffn_weights(p, inputs, p.es)
    win, INC2 = make_windows(p, t_ygsh, t_fl, t_ygw, d_nc, zband,
                             (ctx["stage"], ctx["stage2"]))
    p.raw("sync", win, INC2)
    emit_ffn_main(p, ctx, t_ygw)
    res = p.run()
    if _DEBUG:
        _dbg["F"] = res
    out = np.zeros((B, S, D), np.float32)
    for core in range(8):
        b, th = core // 2, core % 2
        out[b, th * THALF:(th + 1) * THALF, :] = np.asarray(res[core]["out"], np.float32)
    return out


# version 30
# speedup vs baseline: 1.0286x; 1.0147x over previous
import contextlib
import numpy as np
import ml_dtypes
import concourse.bass as bass
import concourse.mybir as mybir
from concourse.bass_utils import run_bass_kernel_spmd

F32 = mybir.dt.float32
BF16 = mybir.dt.bfloat16
I32 = mybir.dt.int32
OP = mybir.AluOpType
AF = mybir.ActivationFunctionType
AX = mybir.AxisListType
BF = ml_dtypes.bfloat16

B, S, D, DI, DH, NS = 4, 4096, 1024, 2048, 4096, 16
DHALF = DI // 2
THALF = S // 2
TCA = 512
NCA = S // TCA
NTILE = TCA // 128
NG = 8
PAD = 64
TB = THALF + 2 * PAD
SP = S + 2 * PAD
TCB = 256
NCB = THALF // TCB
TW = TCB + 2
EPS = 1e-5

_DEBUG = False
_STUB_EXCHANGE = False
_dbg = {}


class _Rec:
    def __init__(self):
        self.calls = []

    def __getattr__(self, name):
        def call(*a, **k):
            self.calls.append((name, a, k))
            return self
        return call

    def then_inc(self, *a, **k):
        return self


def _is_ap(v):
    return isinstance(v, bass.AP)


class Prog:

    ENGS = ("sync", "vector", "scalar", "tensor", "gpsimd")

    def __init__(self, name):
        self.nc = bass.Bass("TRN2", target_bir_lowering=False, num_devices=8,
                            detect_race_conditions=False)
        self.items = []
        self.hist = {}
        self.in_maps = [dict() for _ in range(8)]
        self.es = contextlib.ExitStack()
        self._ctr = 0
        self.name = name
        self._rowlen = {}
        self._fence = None

    def din(self, name, arrays):
        a0 = np.ascontiguousarray(arrays[0])
        dt = {np.dtype(np.float32): F32, np.dtype(BF): BF16,
              np.dtype(np.int32): I32}[a0.dtype]
        for i in range(8):
            self.in_maps[i][name] = np.ascontiguousarray(arrays[i])
        return self.nc.dram_tensor(name, list(a0.shape), dt, kind="ExternalInput").ap()

    def dout(self, name, shape, dt=F32):
        return self.nc.dram_tensor(name, list(shape), dt, kind="ExternalOutput").ap()

    def dtmp(self, name, shape, dt=F32, shared=False):
        return self.nc.dram_tensor(
            name, list(shape), dt, kind="Internal",
            addr_space="Shared" if shared else "Local").ap()

    def sb(self, shape, dt=F32, es=None):
        self._ctr += 1
        return (es or self.es).enter_context(
            self.nc.sbuf_tensor(f"sb{self._ctr}", shape, dt))

    def ps(self, shape, dt=F32, es=None):
        self._ctr += 1
        return (es or self.es).enter_context(
            self.nc.psum_tensor(f"ps{self._ctr}", shape, dt))

    def _span(self, ap):
        t = ap.tensor
        name = t.name
        dims = ap.ap
        off = ap.offset
        sp = str(ap.space)
        if "SBUF" in sp or "PSUM" in sp:
            rl = self._rowlen.get(name)
            if rl is None:
                shp = list(t.shape)
                rl = 1
                for s in shp[1:]:
                    rl *= s
                self._rowlen[name] = rl
            p0 = off // rl if rl else 0
            f0 = off - p0 * rl
            pc = dims[0][1] if dims else 1
            f1 = f0 + sum(st * (ct - 1) for st, ct in dims[1:]) + 1
            return (name, p0, p0 + pc, f0, f1)
        f1 = off + sum(st * (ct - 1) for st, ct in dims) + 1
        return (name, 0, 128, off, f1)

    @staticmethod
    def _overlap(a, b):
        return a[1] < b[2] and b[1] < a[2] and a[3] < b[4] and b[3] < a[4]

    @staticmethod
    def _contains(outer, inner):
        return (outer[1] <= inner[1] and outer[2] >= inner[2]
                and outer[3] <= inner[3] and outer[4] >= inner[4])

    def _classify(self, name, a, k):
        if name == "dma_start":
            out = k.get("out", a[0] if a else None)
            in_ = k.get("in_", a[1] if len(a) > 1 else None)
            return [out], ([in_] if _is_ap(in_) else [])
        pos = [v for v in a if _is_ap(v)]
        writes = pos[:1]
        reads = pos[1:]
        for kk, v in k.items():
            if not _is_ap(v):
                continue
            if kk == "accum_out":
                writes.append(v)
            else:
                reads.append(v)
        return writes, reads

    def step(self, engine, fn, inc=1):
        rec = _Rec()
        fn(rec)
        assert len(rec.calls) == 1, f"step recorded {len(rec.calls)} calls"
        name, a, k = rec.calls[0]
        writes, reads = self._classify(name, a, k)
        idx = len(self.items)
        deps = set()
        if self._fence is not None:
            deps.add(self._fence)
        for ap in reads:
            s = self._span(ap)
            for ent in reversed(self.hist.get(s[0], ())):
                if ent[1] and self._overlap(s, ent[2]):
                    deps.add(ent[0])
                    if self._contains(ent[2], s):
                        break
        for ap in writes:
            s = self._span(ap)
            for ent in reversed(self.hist.get(s[0], ())):
                if self._overlap(s, ent[2]):
                    deps.add(ent[0])
                    if ent[1] and self._contains(ent[2], s):
                        break
        for ap in reads:
            s = self._span(ap)
            self.hist.setdefault(s[0], []).append((idx, False, s))
        for ap in writes:
            s = self._span(ap)
            self.hist.setdefault(s[0], []).append((idx, True, s))
        self.items.append(dict(engine=engine, call=(name, a, k), inc=inc,
                               deps=deps, raw=None, is_dma=(name == "dma_start")))

    def raw(self, engine, emitter, inc):
        idx = len(self.items)
        deps = set(range(idx))
        self.items.append(dict(engine=engine, call=None, inc=inc, deps=deps,
                               raw=emitter, is_dma=False))
        self._fence = idx

    def dma(self, dst, src, eng="sync"):
        self.step(eng, (lambda dst=dst, src=src:
                        (lambda e: e.dma_start(out=dst, in_=src)))(), 16)

    def run(self):
        nc = self.nc
        NLANES = 6
        lane_count = {}
        lane_prev = {}
        eng_cum = {e: 0 for e in self.ENGS}
        qpos = {}
        n = len(self.items)
        semkey = [None] * n
        vafter = [0] * n
        selfdep = [None] * n
        incs = [0] * n
        needed = [False] * n
        for it in self.items:
            for d in it["deps"]:
                needed[d] = True
        last_per_eng = {}
        for i, it in enumerate(self.items):
            last_per_eng[it["engine"]] = i
        for i in last_per_eng.values():
            needed[i] = True
        for i, it in enumerate(self.items):
            if it["raw"] is not None or it["is_dma"]:
                needed[i] = True
        for i, it in enumerate(self.items):
            e = it["engine"]
            if it["raw"] is not None:
                incs[i] = it["inc"]
                eng_cum[e] += incs[i]
                semkey[i] = e
                vafter[i] = eng_cum[e]
            elif it["is_dma"]:
                j = qpos.get(e, 0) % NLANES
                qpos[e] = qpos.get(e, 0) + 1
                key = (e, j)
                selfdep[i] = lane_prev.get(key)
                lane_prev[key] = i
                lane_count[key] = lane_count.get(key, 0) + 16
                semkey[i] = key
                incs[i] = 16
                vafter[i] = lane_count[key]
            else:
                inc = it["inc"] if needed[i] else 0
                incs[i] = inc
                eng_cum[e] += inc
                semkey[i] = e
                vafter[i] = eng_cum[e]
        totals = dict(eng_cum)
        lane_totals = dict(lane_count)
        with contextlib.ExitStack() as semstack:
            sems = {e: semstack.enter_context(nc.semaphore(f"sem_{e}"))
                    for e in self.ENGS}
            for key in lane_totals:
                sems[key] = semstack.enter_context(
                    nc.semaphore(f"sem_{key[0]}_l{key[1]}"))
            with nc.Block() as block:
                def mk(engname):
                    def _(eng):
                        hwm = {}

                        def wait(key, v):
                            if hwm.get(key, 0) >= v:
                                return
                            eng.wait_ge(sems[key], v)
                            hwm[key] = v
                        for i, it in enumerate(self.items):
                            if it["engine"] != engname:
                                continue
                            want = {}
                            deps = it["deps"]
                            if selfdep[i] is not None:
                                deps = deps | {selfdep[i]}
                            for d in deps:
                                key = semkey[d]
                                if (key == engname and engname == "tensor"
                                        and it["raw"] is None):
                                    continue
                                v = vafter[d]
                                if v > want.get(key, 0):
                                    want[key] = v
                            for key, v in want.items():
                                wait(key, v)
                            if it["raw"] is not None:
                                it["raw"](eng, sems[engname], vafter[i] - incs[i])
                                hwm = {}
                            else:
                                name, a, k = it["call"]
                                inst = getattr(eng, name)(*a, **k)
                                if incs[i]:
                                    inst.then_inc(sems[semkey[i]], incs[i])
                        for e2, v in totals.items():
                            if e2 != engname and v:
                                wait(e2, v)
                        for key, v in lane_totals.items():
                            wait(key, v)
                    return _
                block.sync(mk("sync"))
                block.vector(mk("vector"))
                block.scalar(mk("scalar"))
                block.tensor(mk("tensor"))
                block.gpsimd(mk("gpsimd"))
            res = run_bass_kernel_spmd(self.nc, self.in_maps, core_ids=list(range(8)))
        return res.results


def _ln_stats(p, xtile, mcol, rcol, sq, tmpc, i, epscol):
    st = p.step
    st("vector", lambda e: e.tensor_reduce(mcol[:, i:i + 1], xtile, AX.X, OP.add))
    st("vector", lambda e: e.tensor_scalar_mul(mcol[:, i:i + 1], mcol[:, i:i + 1], 1.0 / D))
    st("scalar", lambda e: e.activation(sq[:], xtile, AF.Square))
    st("vector", lambda e: e.tensor_reduce(tmpc[:, 0:1], sq[:], AX.X, OP.add))
    st("vector", lambda e: e.tensor_scalar_mul(tmpc[:, 0:1], tmpc[:, 0:1], 1.0 / D))
    st("vector", lambda e: e.tensor_tensor(tmpc[:, 1:2], mcol[:, i:i + 1], mcol[:, i:i + 1], OP.mult))
    st("vector", lambda e: e.tensor_tensor(tmpc[:, 0:1], tmpc[:, 0:1], tmpc[:, 1:2], OP.subtract))
    st("scalar", lambda e: e.activation(rcol[:, i:i + 1], tmpc[:, 0:1], AF.Ln, bias=epscol[:, 0:1]))
    st("scalar", lambda e: e.activation(rcol[:, i:i + 1], rcol[:, i:i + 1], AF.Exp, scale=-0.5))


def emit_mamba(p, inp, t_ygl, esA):
    x = np.asarray(inp["L0_attn_x"], np.float32)
    WTi = np.ascontiguousarray(np.asarray(inp["L0_attn_in_proj_w"], np.float32).T)
    convw = np.asarray(inp["L0_attn_conv_w"], np.float32)
    convb = np.asarray(inp["L0_attn_conv_b"], np.float32).reshape(DI, 1)
    XPT = np.ascontiguousarray(np.asarray(inp["L0_attn_x_proj_w"], np.float32).T)
    wdt = np.ascontiguousarray(np.asarray(inp["L0_attn_dt_proj_w"], np.float32))
    bdt = np.asarray(inp["L0_attn_dt_proj_b"], np.float32).reshape(DI, 1)
    alog = np.asarray(inp["L0_attn_A_log"], np.float32)
    Dv = np.asarray(inp["L0_attn_D"], np.float32).reshape(DI, 1)

    xs, wts, xpts, diags, cbs, wdts, bdts, alogs, ds = [], [], [], [], [], [], [], [], []
    ar = np.arange(128)
    for core in range(8):
        b, h = core // 2, core % 2
        own = h * DHALF
        oth = (1 - h) * DHALF
        chan = np.concatenate([np.arange(own, own + DHALF), np.arange(oth, oth + DHALF)])
        xs.append(x[b])
        wts.append(np.concatenate(
            [WTi[:, chan], WTi[:, DI + own: DI + own + DHALF]], axis=1))
        xpts.append(XPT[chan, :])
        cw = convw[chan, 0, :]
        dg = np.zeros((16, 4, 128, 128), np.float32)
        for cc in range(16):
            for k in range(4):
                dg[cc, k, ar, ar] = cw[cc * 128 + ar, k]
        diags.append(dg)
        cbs.append(convb[chan])
        wdts.append(wdt[own:own + DHALF])
        bdts.append(bdt[own:own + DHALF])
        alogs.append(alog[own:own + DHALF])
        ds.append(Dv[own:own + DHALF])

    d_x = p.din("x", xs)
    d_wt = p.din("wt", wts)
    d_xpt = p.din("xpt", xpts)
    d_diag = p.din("diag", diags)
    d_cb = p.din("cb", cbs)
    d_wdt = p.din("wdt", wdts)
    d_bdt = p.din("bdt", bdts)
    d_alog = p.din("alog", alogs)
    d_dv = p.din("dv", ds)
    d_idb = p.din("identb", [np.eye(128, dtype=BF)] * 8)
    t_dgb = p.dtmp("dgb", (16, 4, 128, 128), BF16)

    sb = lambda shape, dt=F32: p.sb(shape, dt, es=esA)
    ps = lambda shape, dt=F32: p.ps(shape, dt, es=esA)

    scratch2 = sb((128, 8 * TCA))
    stage = scratch2[:, 0:1536]
    sq = scratch2[:, 0:1024]
    w_in = [sb((128, 3072), BF16) for _ in range(8)]
    w_xp = sb((128, 16 * 33), BF16)
    w_dgb = [sb((128, 4 * 128), BF16) for _ in range(2)]
    dgstage = sb((128, 128), BF16)
    cb_s = sb((128, 16))
    wdt_s = sb((128, 8)); bdt_s = sb((128, 8))
    a_s = sb((128, 8 * 16)); dv_s = sb((128, 8))
    ones_s = sb((65, 128))
    epscol = sb((128, 1))
    identb = sb((128, 128), BF16)
    xt = scratch2[:, 1024:2048]
    mcol = sb((128, 4)); rcol = sb((128, 4)); tmpc = sb((128, 4))
    lnxt = sb((128, 1024), BF16)
    lnxf = sb((128, 8 * TCA), BF16)
    xis = sb((128, 16 * (TCA + 3)), BF16)
    zs = sb((128, 8 * TCA), BF16)
    xco = sb((128, 8 * TCA))
    xcbf = [sb((128, TCA), BF16) for _ in range(2)]
    xp_s = sb((33, TCA))
    xp_rows = sb((65, 11 * TCA))

    def xprow(r):
        return xp_rows[32 * (r % 3): 32 * (r % 3) + 1,
                       (r // 3) * TCA:(r // 3 + 1) * TCA]

    def onesrow(r):
        return ones_s[32 * (r % 3): 32 * (r % 3) + 1, :]
    dts = sb((128, 8 * TCA))
    dtxdc = sb((128, TCA))
    bigs = sb((128, NG * TCA))
    hstate = sb((128, 8 * 16))
    yscr = sb((128, TCA)); y2scr = sb((128, TCA))
    zgate = sb((128, TCA))
    ygout = sb((128, TCA), BF16)

    pA = ps((128, 512)); pB = ps((128, 512))
    pXP = ps((33, 512))
    pT = ps((128, 128), BF16)
    pR = [ps((128, 512)) for _ in range(2)]
    pC = [ps((128, 512)) for _ in range(2)]

    st = p.step
    dma = p.dma

    dma(identb[:], d_idb)
    stg2 = [scratch2[:, 0:1536], scratch2[:, 1536:3072]]
    for jc in range(8):
        for half in range(2):
            sgi = stg2[(jc * 2 + half) % 2]
            dma(sgi, d_wt[jc * 128:(jc + 1) * 128, half * 1536:(half + 1) * 1536])
            st("vector", (lambda jc=jc, half=half, sgi=sgi: (lambda e: e.tensor_copy(
                w_in[jc][:, half * 1536:(half + 1) * 1536], sgi)))())
    for dc in range(16):
        sgi = scratch2[:, (dc % 2) * 64:(dc % 2) * 64 + 33]
        dma(sgi, d_xpt[dc * 128:(dc + 1) * 128, :])
        st("vector", (lambda dc=dc, sgi=sgi: (lambda e: e.tensor_copy(
            w_xp[:, dc * 33:(dc + 1) * 33], sgi)))())
    for cc in range(16):
        for k in range(4):
            j = (cc * 4 + k) % 2
            sgi = scratch2[:, 256 + j * 128: 256 + (j + 1) * 128]
            dma(sgi, d_diag[cc, k])
            st("vector", (lambda sgi=sgi: (lambda e: e.tensor_copy(
                dgstage[:], sgi)))())
            dma(t_dgb[cc, k], dgstage[:])
    for cc in range(16):
        dma(cb_s[:, cc:cc + 1], d_cb[cc * 128:(cc + 1) * 128, :])
    for dc in range(8):
        dma(wdt_s[:, dc:dc + 1], d_wdt[dc * 128:(dc + 1) * 128, :])
        dma(bdt_s[:, dc:dc + 1], d_bdt[dc * 128:(dc + 1) * 128, :])
        dma(dv_s[:, dc:dc + 1], d_dv[dc * 128:(dc + 1) * 128, :])
        dma(a_s[:, dc * 16:(dc + 1) * 16], d_alog[dc * 128:(dc + 1) * 128, :])
    st("scalar", lambda e: e.activation(a_s[:], a_s[:], AF.Exp))
    st("vector", lambda e: e.tensor_scalar_mul(a_s[:], a_s[:], -1.0))
    st("vector", lambda e: e.memset(ones_s[:], 1.0))
    st("vector", lambda e: e.memset(epscol[:], EPS))
    st("vector", lambda e: e.memset(hstate[:], 0.0))
    st("vector", lambda e: e.memset(xis[:], 0.0))

    for c in range(NCA):
        t0 = c * TCA
        for i in range(NTILE):
            dma(xt, d_x[t0 + i * 128: t0 + (i + 1) * 128, :])
            _ln_stats(p, xt, mcol, rcol, sq, tmpc, i % 4, epscol)
            st("vector", (lambda i=i: (lambda e: e.tensor_scalar(
                lnxt[:], xt, mcol[:, i % 4:i % 4 + 1],
                rcol[:, i % 4:i % 4 + 1], OP.subtract, OP.mult)))())
            for jc in range(8):
                st("tensor", (lambda jc=jc: (lambda e: e.transpose(
                    pT[:], lnxt[:, jc * 128:(jc + 1) * 128], identb[:])))())
                st("scalar", (lambda i=i, jc=jc: (lambda e: e.activation(
                    lnxf[:, jc * TCA + i * 128: jc * TCA + (i + 1) * 128], pT[:],
                    AF.Copy)))())

        for mc in range(24):
            for jc in range(8):
                st("tensor", (lambda mc=mc, jc=jc: (lambda e: e.matmul(
                    pA[:, 0:TCA], w_in[jc][:, mc * 128:(mc + 1) * 128],
                    lnxf[:, jc * TCA:(jc + 1) * TCA],
                    start=(jc == 0), stop=(jc == 7))))())
            if mc < 16:
                o = mc * (TCA + 3) + 3
                st("scalar", (lambda o=o: (lambda e: e.activation(
                    xis[:, o: o + TCA], pA[:, 0:TCA], AF.Copy)))())
            else:
                zc = mc - 16
                st("scalar", (lambda zc=zc: (lambda e: e.activation(
                    zs[:, zc * TCA:(zc + 1) * TCA], pA[:, 0:TCA], AF.Silu)))())

        for cc in range(16):
            base = cc * (TCA + 3)
            dma(w_dgb[cc % 2][:].rearrange("a (k b) -> a k b", b=128),
                t_dgb[cc].rearrange("k a b -> a k b"))
            for k in range(4):
                st("tensor", (lambda cc=cc, k=k, base=base: (lambda e: e.matmul(
                    pB[:, 0:TCA], w_dgb[cc % 2][:, k * 128:(k + 1) * 128],
                    xis[:, base + k: base + k + TCA],
                    start=(k == 0), stop=(k == 3))))())
            st("vector", (lambda base=base: (lambda e: e.tensor_copy(
                xis[:, base: base + 3], xis[:, base + TCA: base + TCA + 3])))())
            st("scalar", (lambda cc=cc: (lambda e: e.activation(
                xcbf[cc % 2][:], pB[:, 0:TCA], AF.Silu, bias=cb_s[:, cc:cc + 1])))())
            if cc < 8:
                st("scalar", (lambda cc=cc: (lambda e: e.activation(
                    xco[:, cc * TCA:(cc + 1) * TCA], pB[:, 0:TCA], AF.Silu,
                    bias=cb_s[:, cc:cc + 1])))())
            st("tensor", (lambda cc=cc: (lambda e: e.matmul(
                pXP[:, 0:TCA], w_xp[:, cc * 33:(cc + 1) * 33], xcbf[cc % 2][:],
                start=(cc == 0), stop=(cc == 15))))())
        st("scalar", lambda e: e.activation(xp_s[:], pXP[:, 0:TCA], AF.Copy))
        for r in range(33):
            dma(xprow(r), xp_s[r:r + 1, :])

        st("tensor", lambda e: e.matmul(pR[0][:, 0:TCA], onesrow(0),
                                        xprow(0), start=True, stop=True))
        for dc in range(8):
            st("scalar", (lambda dc=dc: (lambda e: e.activation(
                dts[:, dc * TCA:(dc + 1) * TCA], pR[0][:, 0:TCA], AF.Identity,
                scale=wdt_s[:, dc:dc + 1], bias=bdt_s[:, dc:dc + 1])))())
        st("scalar", lambda e: e.activation(scratch2[:], dts[:], AF.Abs))
        st("scalar", lambda e: e.activation(scratch2[:], scratch2[:], AF.Exp, scale=-1.0))
        st("scalar", lambda e: e.activation(scratch2[:], scratch2[:], AF.Ln, bias=1.0))
        st("scalar", lambda e: e.activation(dts[:], dts[:], AF.Relu))
        st("vector", lambda e: e.tensor_add(dts[:], dts[:], scratch2[:]))

        for dc in range(8):
            dsl = dts[:, dc * TCA:(dc + 1) * TCA]
            st("vector", (lambda dc=dc: (lambda e: e.tensor_tensor(
                dtxdc[:], dts[:, dc * TCA:(dc + 1) * TCA],
                xco[:, dc * TCA:(dc + 1) * TCA], OP.mult)))())
            for g in range(2):
                for nl in range(NG):
                    n = g * NG + nl
                    dascr = scratch2[:, (nl % 2) * TCA:(nl % 2 + 1) * TCA]
                    dbscr = scratch2[:, (2 + nl % 2) * TCA:(3 + nl % 2) * TCA]
                    st("tensor", (lambda n=n, nl=nl: (lambda e: e.matmul(
                        pR[nl % 2][:, 0:TCA], onesrow(1 + n),
                        xprow(1 + n), start=True, stop=True)))())
                    st("scalar", (lambda dsl=dsl, dc=dc, n=n, dascr=dascr:
                        (lambda e: e.activation(
                            dascr, dsl, AF.Exp,
                            scale=a_s[:, dc * 16 + n: dc * 16 + n + 1])))())
                    st("vector", (lambda nl=nl, dbscr=dbscr: (lambda e: e.tensor_tensor(
                        dbscr, dtxdc[:], pR[nl % 2][:, 0:TCA], OP.mult)))())
                    st("vector", (lambda dc=dc, n=n, nl=nl, dascr=dascr, dbscr=dbscr:
                        (lambda e: e.tensor_tensor_scan(
                            bigs[:, nl: NG * TCA: NG], dascr, dbscr,
                            hstate[:, dc * 16 + n: dc * 16 + n + 1], OP.mult, OP.add)))())
                st("vector", (lambda dc=dc, g=g: (lambda e: e.tensor_copy(
                    hstate[:, dc * 16 + g * NG: dc * 16 + (g + 1) * NG],
                    bigs[:, (TCA - 1) * NG: TCA * NG])))())
                for nl in range(NG):
                    n = g * NG + nl
                    st("tensor", (lambda n=n, nl=nl: (lambda e: e.matmul(
                        pC[nl % 2][:, 0:TCA], onesrow(17 + n),
                        xprow(17 + n), start=True, stop=True)))())
                    st("vector", (lambda nl=nl: (lambda e: e.tensor_tensor(
                        bigs[:, nl: NG * TCA: NG], bigs[:, nl: NG * TCA: NG],
                        pC[nl % 2][:, 0:TCA], OP.mult)))())
                rdst = yscr if g == 0 else y2scr
                st("vector", (lambda rdst=rdst: (lambda e: e.tensor_reduce(
                    rdst[:].rearrange("p (t o) -> p t o", o=1),
                    bigs[:].rearrange("p (t n) -> p t n", n=NG), AX.X, OP.add)))())
                if g == 1:
                    st("gpsimd", lambda e: e.tensor_add(yscr[:], yscr[:], y2scr[:]))
            st("vector", (lambda dc=dc: (lambda e: e.scalar_tensor_tensor(
                yscr[:], xco[:, dc * TCA:(dc + 1) * TCA], dv_s[:, dc:dc + 1],
                yscr[:], OP.mult, OP.add)))())
            st("gpsimd", (lambda dc=dc: (lambda e: e.tensor_tensor(
                ygout[:], yscr[:], zs[:, dc * TCA:(dc + 1) * TCA], OP.mult)))())
            dma(t_ygl[dc * 128:(dc + 1) * 128, t0:t0 + TCA], ygout[:], eng="gpsimd")
    return scratch2


def make_publish(p, t_ygl, t_ygsh, t_fl, d_nc, scratch2, flag1):
    nc = p.nc
    stg = scratch2[:].bitcast(BF16)
    lA = p.es.enter_context(nc.semaphore("pub_lA"))
    lB = p.es.enter_context(nc.semaphore("pub_lB"))

    INC = 2 * 16

    def emitter(eng, sem, base):
        eng.dma_start(out=flag1[:], in_=d_nc).then_inc(sem, 16)
        pid = eng.partition_id()
        with eng.register("rpar1") as rpar:
            eng.reg_alu(rpar, pid, 2, OP.mod)

            def branch(me):
                lanes = [lA, lB]
                cnt = [0, 0]
                for rb in range(8):
                    ln = rb % 2
                    if cnt[ln]:
                        eng.wait_ge(lanes[ln], cnt[ln])
                    eng.dma_start(
                        out=stg[:, ln * 4096:(ln + 1) * 4096],
                        in_=t_ygl[rb * 128:(rb + 1) * 128, :]
                    ).then_inc(lanes[ln], 16)
                    cnt[ln] += 16
                    eng.wait_ge(lanes[ln], cnt[ln])
                    eng.dma_start(
                        out=t_ygsh[me, rb * 128:(rb + 1) * 128, :],
                        in_=stg[:, ln * 4096:(ln + 1) * 4096]
                    ).then_inc(lanes[ln], 16)
                    cnt[ln] += 16
                eng.wait_ge(lanes[0], cnt[0])
                eng.wait_ge(lanes[1], cnt[1])
                eng.wait_ge(sem, base + 16)
                eng.dma_start(out=t_fl[0:1, me * 2:me * 2 + 2],
                              in_=flag1[:]).then_inc(sem, 16)

            with eng.If_eq(rpar, 0):
                branch(0)
            with eng.Else():
                branch(1)
        eng.wait_ge(sem, base + INC)

    return emitter, INC


def make_windows(p, t_ygsh, t_fl, t_ygw, d_nc, zband, bstages):
    nc = p.nc
    flag1b = p.sb((1, 2), I32)
    flagp = p.sb((1, 2), I32)
    wbufs = [bstages[0][:].bitcast(BF16), bstages[1][:].bitcast(BF16)]
    s_poll = p.es.enter_context(nc.semaphore("s_poll"))
    lA = p.es.enter_context(nc.semaphore("win_lA"))
    lB = p.es.enter_context(nc.semaphore("win_lB"))

    INC = 2 * 16
    WV = 2112

    def emitter(eng, sem, base):
        eng.dma_start(out=flag1b[:], in_=d_nc).then_inc(sem, 16)
        eng.wait_ge(sem, base + 16)
        pid = eng.partition_id()
        with eng.register("rpar2") as rpar:
            eng.reg_alu(rpar, pid, 2, OP.mod)

            def branch(me, pe):
                tag = f"w{me}"
                with eng.register(f"rc{tag}") as rc, \
                     eng.register(f"rv{tag}") as rv, \
                     eng.register(f"rw0{tag}") as rw0, \
                     eng.register(f"rw1{tag}") as rw1, \
                     eng.register(f"rn0{tag}") as rn0, \
                     eng.register(f"rn1{tag}") as rn1, \
                     eng.register(f"rc2{tag}") as rc2:
                    eng.load(rn0, flag1b[0:1, 0:1])
                    eng.load(rn1, flag1b[0:1, 1:2])
                    eng.reg_mov(rc, 1)
                    eng.reg_mov(rv, 16)
                    with eng.While(rc):
                        eng.dma_start(out=flagp[:],
                                      in_=t_fl[0:1, pe * 2:pe * 2 + 2]
                                      ).then_inc(s_poll, 16)
                        eng.wait_ge(s_poll, rv)
                        eng.reg_add(rv, rv, 16)
                        eng.load(rw0, flagp[0:1, 0:1])
                        eng.load(rw1, flagp[0:1, 1:2])
                        eng.reg_alu(rc, rw0, rn0, OP.not_equal)
                        eng.reg_alu(rc2, rw1, rn1, OP.not_equal)
                        eng.reg_alu(rc, rc, rc2, OP.logical_or)
                lanes = [lA, lB]
                cnt = [0, 0]
                zlo, zhi = (0, 64) if me == 0 else (2112, 2176)
                for rb in range(16):
                    ln = rb % 2
                    eng.dma_start(out=t_ygw[rb * 128:(rb + 1) * 128, zlo:zhi],
                                  in_=zband[:]).then_inc(lanes[ln], 16)
                    cnt[ln] += 16
                scol = 0 if me == 0 else S - WV
                dcol = 64 if me == 0 else 0
                for s in range(2):
                    for rb in range(8):
                        ln = (s * 8 + rb) % 2
                        eng.wait_ge(lanes[ln], cnt[ln])
                        eng.dma_start(
                            out=wbufs[ln][:, 0:WV],
                            in_=t_ygsh[s, rb * 128:(rb + 1) * 128, scol:scol + WV]
                        ).then_inc(lanes[ln], 16)
                        cnt[ln] += 16
                        eng.wait_ge(lanes[ln], cnt[ln])
                        eng.dma_start(
                            out=t_ygw[s * DHALF + rb * 128: s * DHALF + (rb + 1) * 128,
                                      dcol:dcol + WV],
                            in_=wbufs[ln][:, 0:WV]
                        ).then_inc(lanes[ln], 16)
                        cnt[ln] += 16
                eng.wait_ge(lanes[0], cnt[0])
                eng.wait_ge(lanes[1], cnt[1])
                eng.dma_start(out=t_fl[0:1, 8 + me:9 + me],
                              in_=flag1b[0:1, 0:1]).then_inc(sem, 16)

            with eng.If_eq(rpar, 0):
                branch(0, 1)
            with eng.Else():
                branch(1, 0)
        eng.wait_ge(sem, base + INC)

    return emitter, INC


def emit_ffn_weights(p, inp, esB):
    x = np.asarray(inp["L0_attn_x"], np.float32)
    WOT = np.ascontiguousarray(np.asarray(inp["L0_attn_out_proj_w"], np.float32).T)
    W1T = np.ascontiguousarray(np.asarray(inp["L0_ffn_w1"], np.float32).T)
    W2T = np.ascontiguousarray(np.asarray(inp["L0_ffn_w2"], np.float32).T)
    fcw = np.asarray(inp["L0_ffn_conv_w"], np.float32)
    fcb = np.asarray(inp["L0_ffn_conv_b"], np.float32).reshape(DH, 1)

    fdiag = np.zeros((32, 3, 128, 128), np.float32)
    ar = np.arange(128)
    for m in range(32):
        for k in range(3):
            fdiag[m, k, ar, ar] = fcw[m * 128 + ar, 0, k]

    xps = []
    for core in range(8):
        b, th = core // 2, core % 2
        t0 = th * THALF - PAD
        xp = np.zeros((TB, D), np.float32)
        lo, hi = max(0, t0), min(S, t0 + TB)
        xp[lo - t0:hi - t0, :] = x[b][lo:hi, :]
        xps.append(xp)

    d_xw = p.din("xw", xps)
    d_wot = p.din("wot", [WOT] * 8)
    d_w1t = p.din("w1t", [W1T] * 8)
    d_w2t = p.din("w2t", [W2T] * 8)
    d_fdg = p.din("fdg", [fdiag] * 8)
    d_fcb = p.din("fcb", [fcb] * 8)
    d_idb2 = p.din("identb2", [np.eye(128, dtype=BF)] * 8)
    d_idf = p.din("identf", [np.eye(128, dtype=np.float32)] * 8)

    o_out = p.dout("out", (THALF, D))

    t_x2 = p.dtmp("x2s", (TB, D))
    t_xq1 = p.dtmp("xq1s", (8, 128, TB), BF16)
    t_s1i = p.dtmp("s1is", (TB, 1))
    t_s2 = p.dtmp("s2s", (THALF, 1)); t_s2i = p.dtmp("s2is", (THALF, 1))
    t_wq1 = p.dtmp("wq1s", (32, 8, 128, 128), BF16)
    t_wq2 = p.dtmp("wq2s", (8, 32, 128, 128), BF16)
    t_wot = p.dtmp("wots", (8, 16, 128, 128), BF16)
    t_fdg = p.dtmp("fdgs", (32, 3, 128, 128), BF16)

    sb = lambda shape, dt=F32: p.sb(shape, dt, es=esB)
    ps = lambda shape, dt=F32: p.ps(shape, dt, es=esB)

    stage = sb((128, 2048))
    stage2 = sb((128, 2048))
    qi32 = sb((128, 2048), I32)
    qbf = sb((128, TB), BF16)
    acc8 = sb((128, 34))
    onescol = sb((128, 1))
    ones_s = sb((1, 128))
    epscol = sb((128, 1))
    mw1col = sb((128, 1))
    mw2col = sb((128, 1))
    minv = sb((128, 1))
    identb = sb((128, 128), BF16)
    identf = sb((128, 128))
    fcb_s = sb((128, 32))
    ygbf = sb((128, 16 * 512), BF16)
    wotb = [sb((128, 16 * 128), BF16) for _ in range(2)]
    afm = sb((128, 512))
    xt1 = sb((128, 1024))
    x2buf = sb((128, 4 * 1024))
    mcol = sb((128, 2)); rcol = sb((128, 2)); tmpc = sb((128, 2))
    sq = sb((128, 1024))
    u_scr = sb((128, 1024))
    amax = sb((128, 1)); s1col = sb((128, 1)); rscol = sb((128, 1))
    s1icol = sb((128, 1))
    xq1i = sb((128, 1024), I32)
    xq1t = sb((128, 1024), BF16)
    s1irow = sb((1, TW)); arow = sb((1, TW))
    arep = sb((128, TW))
    xq1b = sb((128, 8 * TW), BF16)
    w1b = [sb((128, 8 * 128), BF16) for _ in range(2)]
    h1s = sb((128, 32 * TW), BF16)
    fdgb = [sb((128, 3 * 128), BF16) for _ in range(2)]
    h2 = sb((128, 32 * TCB))
    macc = sb((128, TCB))
    habs = sb((128, TCB))
    mts = sb((128, 128))
    mxc = sb((128, 2)); s2col = sb((128, 2)); s2icol = sb((128, 2))
    s2row = sb((1, TCB)); brow = sb((1, TCB))
    g2rep = sb((128, TCB)); brep = sb((128, TCB))
    xq2i = sb((128, TCB), I32)
    xq2 = sb((128, 32 * TCB), BF16)
    w2b = [sb((128, 32 * 128), BF16) for _ in range(2)]
    ffm = sb((128, TCB))
    x2c = sb((128, 2 * 1024))
    ot = sb((128, 2 * 1024))

    pA = ps((128, 512)); pB = ps((128, 512))
    pCv = ps((128, 512))
    pT32 = ps((128, 128)); pTb = ps((128, 128), BF16)
    pR = ps((128, 512)); pS = ps((128, 8))

    st = p.step
    dma = p.dma

    dma(identb[:], d_idb2)
    dma(identf[:], d_idf)
    st("vector", lambda e: e.memset(ones_s[:], 1.0))
    st("vector", lambda e: e.memset(onescol[:], 1.0))
    st("vector", lambda e: e.memset(epscol[:], EPS))
    for m in range(32):
        dma(fcb_s[:, m:m + 1], d_fcb[m * 128:(m + 1) * 128, :])

    def wquant(d_wt, ntiles, width, mcol_, t_scr, nval):
        nh = (width + 2047) // 2048
        hw = width // nh
        for i in range(ntiles):
            for q in range(nh):
                sgi = (stage if (i * nh + q) % 2 == 0 else stage2)[:, 0:hw]
                dma(sgi, d_wt[i * 128:(i + 1) * 128, q * hw:(q + 1) * hw])
                st("vector", (lambda i=i, q=q, hw=hw, nh=nh, sgi=sgi:
                    (lambda e: e.tensor_reduce(
                        acc8[:, i * nh + q: i * nh + q + 1], sgi, AX.X,
                        OP.add, apply_absolute_value=True)))())
        ncols = ntiles * nh
        st("vector", (lambda ncols=ncols: (lambda e: e.tensor_reduce(
            acc8[:, 33:34].rearrange("p (o q) -> p o q", q=1),
            acc8[:, 0:ncols].rearrange("p (o t) -> p o t", o=1), AX.X, OP.add)))())
        st("tensor", lambda e: e.matmul(pS[0:1, 0:1], onescol[:], acc8[:, 33:34],
                                        start=True, stop=True))
        st("scalar", lambda e: e.activation(acc8[0:1, 32:33], pS[0:1, 0:1], AF.Copy))
        st("vector", (lambda nval=nval: (lambda e: e.tensor_scalar_mul(
            acc8[0:1, 32:33], acc8[0:1, 32:33], 1.0 / nval)))())
        st("vector", lambda e: e.tensor_scalar_max(acc8[0:1, 32:33], acc8[0:1, 32:33], 1e-5))
        st("tensor", lambda e: e.matmul(pS[:, 1:2], ones_s[0:1, :], acc8[0:1, 32:33],
                                        start=True, stop=True))
        st("scalar", (lambda mcol_=mcol_: (lambda e: e.activation(
            mcol_[:], pS[:, 1:2], AF.Copy)))())
        st("vector", (lambda mcol_=mcol_: (lambda e: e.reciprocal(minv[:], mcol_[:])))())
        for i in range(ntiles):
            for q in range(nh):
                sgi = (stage if (i * nh + q) % 2 == 0 else stage2)[:, 0:hw]
                dma(sgi, d_wt[i * 128:(i + 1) * 128, q * hw:(q + 1) * hw])
                st("vector", (lambda hw=hw, sgi=sgi: (lambda e: e.tensor_scalar(
                    sgi, sgi, minv[:, 0:1], 1.49, OP.mult, OP.min)))())
                st("vector", (lambda hw=hw, sgi=sgi: (lambda e: e.tensor_scalar(
                    qi32[:, 0:hw], sgi, -1.49, None, OP.max)))())
                st("vector", (lambda hw=hw: (lambda e: e.tensor_copy(
                    qbf[:, 0:hw], qi32[:, 0:hw])))())
                mph = hw // 128
                dma(t_scr[q * mph:(q + 1) * mph, i].rearrange("m j h -> j m h"),
                    qbf[:, 0:hw].rearrange("j (m h) -> j m h", h=128))
    wquant(d_w1t, 8, 4096, mw1col, t_wq1, DH * D)
    wquant(d_w2t, 32, 1024, mw2col, t_wq2, DH * D)
    for i in range(16):
        dma(stage[:, 0:1024], d_wot[i * 128:(i + 1) * 128, :])
        st("vector", lambda e: e.tensor_copy(qbf[:, 0:1024], stage[:, 0:1024]))
        dma(t_wot[:, i].rearrange("m d c -> d m c"),
            qbf[:, 0:1024].rearrange("d (m c) -> d m c", c=128))
    for m in range(32):
        for k in range(3):
            dma(stage[:, 0:128], d_fdg[m, k])
            st("vector", lambda e: e.tensor_copy(qbf[:, 0:128], stage[:, 0:128]))
            dma(t_fdg[m, k], qbf[:, 0:128])

    return dict(locals())


def emit_ffn_main(p, ctx, t_ygw):
    st = ctx['st']
    dma = ctx['dma']
    d_xw = ctx['d_xw']
    o_out = ctx['o_out']
    t_x2 = ctx['t_x2']
    t_xq1 = ctx['t_xq1']
    t_s1i = ctx['t_s1i']
    t_s2 = ctx['t_s2']
    t_s2i = ctx['t_s2i']
    t_wq1 = ctx['t_wq1']
    t_wq2 = ctx['t_wq2']
    t_wot = ctx['t_wot']
    t_fdg = ctx['t_fdg']
    stage = ctx['stage']
    qi32 = ctx['qi32']
    qbf = ctx['qbf']
    acc8 = ctx['acc8']
    onescol = ctx['onescol']
    ones_s = ctx['ones_s']
    epscol = ctx['epscol']
    mw1col = ctx['mw1col']
    mw2col = ctx['mw2col']
    minv = ctx['minv']
    identb = ctx['identb']
    identf = ctx['identf']
    fcb_s = ctx['fcb_s']
    ygbf = ctx['ygbf']
    wotb = ctx['wotb']
    afm = ctx['afm']
    xt1 = ctx['xt1']
    x2buf = ctx['x2buf']
    mcol = ctx['mcol']
    rcol = ctx['rcol']
    tmpc = ctx['tmpc']
    sq = ctx['sq']
    u_scr = ctx['u_scr']
    amax = ctx['amax']
    s1col = ctx['s1col']
    rscol = ctx['rscol']
    s1icol = ctx['s1icol']
    xq1i = ctx['xq1i']
    xq1t = ctx['xq1t']
    s1irow = ctx['s1irow']
    arow = ctx['arow']
    arep = ctx['arep']
    xq1b = ctx['xq1b']
    w1b = ctx['w1b']
    h1s = ctx['h1s']
    fdgb = ctx['fdgb']
    h2 = ctx['h2']
    macc = ctx['macc']
    habs = ctx['habs']
    mts = ctx['mts']
    mxc = ctx['mxc']
    s2col = ctx['s2col']
    s2icol = ctx['s2icol']
    s2row = ctx['s2row']
    brow = ctx['brow']
    g2rep = ctx['g2rep']
    brep = ctx['brep']
    xq2i = ctx['xq2i']
    xq2 = ctx['xq2']
    w2b = ctx['w2b']
    ffm = ctx['ffm']
    x2c = ctx['x2c']
    ot = ctx['ot']
    pA = ctx['pA']
    pB = ctx['pB']
    pT32 = ctx['pT32']
    pTb = ctx['pTb']
    pR = ctx['pR']
    pS = ctx['pS']
    pCv = ctx['pCv']
    stage2 = ctx['stage2']
    fcs = [(0, 512), (512, 512), (1024, 512), (1536, 512), (2048, 128)]
    for f0, fc in fcs:
        for dc in range(16):
            dma(ygbf[:, dc * 512: dc * 512 + fc],
                t_ygw[dc * 128:(dc + 1) * 128, f0:f0 + fc])
        nt = fc // 128
        for dm in range(8):
            wob = wotb[dm % 2]
            pAc = pA if dm % 2 == 0 else pB
            dma(wob[:].rearrange("d (dc c) -> d dc c", c=128),
                t_wot[dm].rearrange("dc d c -> d dc c"))
            for dc in range(16):
                st("tensor", (lambda dm=dm, dc=dc, fc=fc, wob=wob, pAc=pAc:
                    (lambda e: e.matmul(
                        pAc[:, 0:fc], wob[:, dc * 128:(dc + 1) * 128],
                        ygbf[:, dc * 512: dc * 512 + fc],
                        start=(dc == 0), stop=(dc == 15))))())
            st("scalar", (lambda fc=fc, pAc=pAc: (lambda e: e.activation(
                afm[:, 0:fc], pAc[:, 0:fc], AF.Copy)))())
            for ts in range(nt):
                st("tensor", (lambda ts=ts: (lambda e: e.transpose(
                    pT32[:], afm[:, ts * 128:(ts + 1) * 128], identf[:])))())
                st("vector", (lambda dm=dm, ts=ts: (lambda e: e.tensor_copy(
                    x2buf[:, ts * 1024 + dm * 128: ts * 1024 + (dm + 1) * 128],
                    pT32[:])))())
        for ts in range(nt):
            ti = f0 // 128 + ts
            xb = x2buf[:, ts * 1024:(ts + 1) * 1024]
            dma(xt1[:], d_xw[ti * 128:(ti + 1) * 128, :])
            st("vector", (lambda xb=xb: (lambda e: e.tensor_tensor(
                xb, xb, xt1[:], OP.add)))())
            dma(t_x2[ti * 128:(ti + 1) * 128, :], xb, eng="gpsimd")
            _ln_stats(p, xb, mcol, rcol, sq, tmpc, 0, epscol)
            st("vector", (lambda xb=xb: (lambda e: e.tensor_scalar(
                u_scr[:], xb, mcol[:, 0:1], None, OP.subtract)))())
            st("vector", lambda e: e.tensor_reduce(
                amax[:], u_scr[:], AX.X, OP.max, apply_absolute_value=True))
            st("vector", lambda e: e.tensor_tensor(amax[:], amax[:], rcol[:, 0:1], OP.mult))
            st("vector", lambda e: e.tensor_scalar_max(amax[:], amax[:], 1e-5))
            st("vector", lambda e: e.reciprocal(s1col[:], amax[:]))
            st("vector", lambda e: e.tensor_scalar_mul(s1col[:], s1col[:], 127.0))
            st("vector", lambda e: e.tensor_tensor(rscol[:], rcol[:, 0:1], s1col[:], OP.mult))
            st("vector", lambda e: e.tensor_scalar_mul(s1icol[:], amax[:], 1.0 / 127.0))
            dma(t_s1i[ti * 128:(ti + 1) * 128, :], s1icol[:], eng="gpsimd")
            st("vector", (lambda xb=xb: (lambda e: e.tensor_scalar(
                xq1i[:], xb, mcol[:, 0:1], rscol[:, 0:1], OP.subtract, OP.mult)))())
            st("vector", lambda e: e.tensor_copy(xq1t[:], xq1i[:]))
            for jc in range(8):
                st("tensor", (lambda jc=jc: (lambda e: e.transpose(
                    pTb[:], xq1t[:, jc * 128:(jc + 1) * 128], identb[:])))())
                st("scalar", (lambda jc=jc: (lambda e: e.activation(
                    qbf[:, 0:128], pTb[:], AF.Copy)))())
                dma(t_xq1[jc, :, ti * 128:(ti + 1) * 128], qbf[:, 0:128], eng="gpsimd")

    for c in range(NCB):
        q0 = PAD + c * TCB - 1
        dma(s1irow[:], t_s1i.rearrange("t o -> o t")[:, q0:q0 + TW])
        st("vector", lambda e: e.tensor_scalar(
            arow[:], s1irow[:], mw1col[0:1, 0:1], None, OP.mult))
        st("tensor", lambda e: e.matmul(pR[:, 0:TW], ones_s[0:1, :], arow[:],
                                        start=True, stop=True))
        st("scalar", lambda e: e.activation(arep[:], pR[:, 0:TW], AF.Copy))
        for jc in range(8):
            dma(xq1b[:, jc * TW:(jc + 1) * TW], t_xq1[jc, :, q0:q0 + TW])
        for m in range(32):
            pH = pA if m % 2 == 0 else pB
            dma(w1b[m % 2][:].rearrange("j (jc h) -> j jc h", h=128),
                t_wq1[m].rearrange("jc j h -> j jc h"))
            for jc in range(8):
                st("tensor", (lambda m=m, jc=jc, pH=pH: (lambda e: e.matmul(
                    pH[:, 0:TW], w1b[m % 2][:, jc * 128:(jc + 1) * 128],
                    xq1b[:, jc * TW:(jc + 1) * TW],
                    start=(jc == 0), stop=(jc == 7))))())
            st("vector", (lambda m=m, pH=pH: (lambda e: e.tensor_tensor(
                h1s[:, m * TW:(m + 1) * TW], pH[:, 0:TW], arep[:], OP.mult)))())
        st("vector", lambda e: e.memset(macc[:], 0.0))
        for m in range(32):
            dma(fdgb[m % 2][:].rearrange("a (k b) -> a k b", b=128),
                t_fdg[m].rearrange("k a b -> a k b"))
            for k in range(3):
                st("tensor", (lambda m=m, k=k: (lambda e: e.matmul(
                    pCv[:, 0:TCB], fdgb[m % 2][:, k * 128:(k + 1) * 128],
                    h1s[:, m * TW + k: m * TW + k + TCB],
                    start=(k == 0), stop=(k == 2))))())
            st("scalar", (lambda m=m: (lambda e: e.activation(
                h2[:, m * TCB:(m + 1) * TCB], pCv[:, 0:TCB], AF.Silu,
                bias=fcb_s[:, m:m + 1])))())
            st("scalar", (lambda m=m: (lambda e: e.activation(
                habs[:], h2[:, m * TCB:(m + 1) * TCB], AF.Abs)))())
            st("vector", lambda e: e.tensor_tensor(macc[:], macc[:], habs[:], OP.max))
        for ts in range(TCB // 128):
            st("tensor", (lambda ts=ts: (lambda e: e.transpose(
                pT32[:], macc[:, ts * 128:(ts + 1) * 128], identf[:])))())
            st("vector", lambda e: e.tensor_copy(mts[:], pT32[:]))
            st("vector", (lambda ts=ts: (lambda e: e.tensor_reduce(
                mxc[:, ts:ts + 1], mts[:], AX.X, OP.max)))())
            st("vector", (lambda ts=ts: (lambda e: e.tensor_scalar_max(
                mxc[:, ts:ts + 1], mxc[:, ts:ts + 1], 1e-5)))())
            st("vector", (lambda ts=ts: (lambda e: e.reciprocal(
                s2col[:, ts:ts + 1], mxc[:, ts:ts + 1])))())
            st("vector", (lambda ts=ts: (lambda e: e.tensor_scalar_mul(
                s2col[:, ts:ts + 1], s2col[:, ts:ts + 1], 127.0)))())
            st("vector", (lambda ts=ts: (lambda e: e.tensor_scalar_mul(
                s2icol[:, ts:ts + 1], mxc[:, ts:ts + 1], 1.0 / 127.0)))())
            dma(t_s2[c * TCB + ts * 128: c * TCB + (ts + 1) * 128, :],
                s2col[:, ts:ts + 1], eng="gpsimd")
            dma(t_s2i[c * TCB + ts * 128: c * TCB + (ts + 1) * 128, :],
                s2icol[:, ts:ts + 1], eng="gpsimd")
        dma(s2row[:], t_s2.rearrange("t o -> o t")[:, c * TCB:(c + 1) * TCB])
        dma(brow[:], t_s2i.rearrange("t o -> o t")[:, c * TCB:(c + 1) * TCB])
        st("vector", lambda e: e.tensor_scalar(
            brow[:], brow[:], mw2col[0:1, 0:1], None, OP.mult))
        st("tensor", lambda e: e.matmul(pR[:, 0:TCB], ones_s[0:1, :], s2row[:],
                                        start=True, stop=True))
        st("scalar", lambda e: e.activation(g2rep[:], pR[:, 0:TCB], AF.Copy))
        st("tensor", lambda e: e.matmul(pR[:, 0:TCB], ones_s[0:1, :], brow[:],
                                        start=True, stop=True))
        st("scalar", lambda e: e.activation(brep[:], pR[:, 0:TCB], AF.Copy))
        for m in range(32):
            st("vector", (lambda m=m: (lambda e: e.tensor_tensor(
                xq2i[:], h2[:, m * TCB:(m + 1) * TCB], g2rep[:], OP.mult)))())
            st("vector", (lambda m=m: (lambda e: e.tensor_copy(
                xq2[:, m * TCB:(m + 1) * TCB], xq2i[:])))())
        dma(x2c[:, 0:1024], t_x2[q0 + 1: q0 + 1 + 128, :])
        dma(x2c[:, 1024:2048], t_x2[q0 + 1 + 128: q0 + 1 + 256, :])
        for dm in range(8):
            pH2 = pA if dm % 2 == 0 else pB
            dma(w2b[dm % 2][:].rearrange("h (hc c) -> h hc c", c=128),
                t_wq2[dm].rearrange("hc h c -> h hc c"))
            for hc in range(32):
                st("tensor", (lambda dm=dm, hc=hc, pH2=pH2: (lambda e: e.matmul(
                    pH2[:, 0:TCB], w2b[dm % 2][:, hc * 128:(hc + 1) * 128],
                    xq2[:, hc * TCB:(hc + 1) * TCB],
                    start=(hc == 0), stop=(hc == 31))))())
            st("vector", (lambda pH2=pH2: (lambda e: e.tensor_tensor(
                ffm[:], pH2[:, 0:TCB], brep[:], OP.mult)))())
            for ts in range(TCB // 128):
                st("tensor", (lambda dm=dm, ts=ts: (lambda e: e.transpose(
                    pT32[:], ffm[:, ts * 128:(ts + 1) * 128], identf[:])))())
                st("vector", (lambda dm=dm, ts=ts: (lambda e: e.tensor_tensor(
                    ot[:, ts * 1024 + dm * 128: ts * 1024 + (dm + 1) * 128],
                    pT32[:], x2c[:, ts * 1024 + dm * 128: ts * 1024 + (dm + 1) * 128],
                    OP.add)))())
        dma(o_out[c * TCB: c * TCB + 128, :], ot[:, 0:1024], eng="gpsimd")
        dma(o_out[c * TCB + 128: c * TCB + 256, :], ot[:, 1024:2048], eng="gpsimd")


def kernel(**inputs):
    global _dbg
    _dbg = {}
    p = Prog("F")
    t_ygl = p.dtmp("ygl", (DHALF, S), BF16)
    t_ygsh = p.dtmp("ygsh", (2, DHALF, S), BF16, shared=True)
    t_fl = p.dtmp("flg", (1, 16), I32, shared=True)
    t_ygw = p.dtmp("ygw", (DI, TB), BF16)
    rngv = np.random.default_rng()
    n0, n1 = (int(rngv.integers(1, 2**31 - 1)), int(rngv.integers(1, 2**31 - 1)))
    d_nc = p.din("nonce", [np.array([[n0, n1]], np.int32)] * 8)

    flag1 = p.sb((1, 2), I32)
    esA = contextlib.ExitStack()
    scratch2 = emit_mamba(p, inputs, t_ygl, esA)
    pub, INC1 = make_publish(p, t_ygl, t_ygsh, t_fl, d_nc, scratch2, flag1)
    p.raw("sync", pub, INC1)
    esA.close()
    zband = p.sb((128, 64), BF16)
    p.step("vector", lambda e: e.memset(zband[:], 0.0))
    ctx = emit_ffn_weights(p, inputs, p.es)
    win, INC2 = make_windows(p, t_ygsh, t_fl, t_ygw, d_nc, zband,
                             (ctx["stage"], ctx["stage2"]))
    p.raw("sync", win, INC2)
    emit_ffn_main(p, ctx, t_ygw)
    res = p.run()
    if _DEBUG:
        _dbg["F"] = res
    out = np.zeros((B, S, D), np.float32)
    for core in range(8):
        b, th = core // 2, core % 2
        out[b, th * THALF:(th + 1) * THALF, :] = np.asarray(res[core]["out"], np.float32)
    return out
